# version 13
# baseline (speedup 1.0000x reference)
import numpy as np
import ml_dtypes
import concourse.bass as bass
import concourse.mybir as mybir
from concourse.bass_utils import run_bass_kernel_spmd

F32 = mybir.dt.float32
BF16 = mybir.dt.bfloat16
FP16 = mybir.dt.float16
AF = mybir.ActivationFunctionType
ALU = mybir.AluOpType
AX = mybir.AxisListType

B, T, D = 16, 2048, 1024
NCORES = 8
SPC = B // NCORES
HD, NH, NPAIR = 64, 16, 8
HALF, E = 1536, 8
NSUP = T // 512
NCH = D // 128
NTT = T // 128

GROUP = 16
RDEPTH = 2
NCHUNK = 6720
NG = NCHUNK // GROUP

BF = ml_dtypes.bfloat16

CFG = {"seqs": SPC, "layers": 2, "attn": True, "outproj": True, "ffn": True,
       "pairs": NPAIR, "dbg": False, "olora": True, "dbgh": False}


class Plan:
    OWN = {"t": "t", "v": "v", "s": "s"}

    def __init__(self, sems):
        self.sems = sems
        self.q = {k: [] for k in "tvsgy"}
        self.cnt = {k: 0 for k in sems}
        self.lastw = {e: {} for e in "tvsgy"}

    def emit(self, e, fn):
        rec = [fn, []]
        self.q[e].append(rec)
        return rec

    def inc(self, rec, sem, amt=1):
        self.cnt[sem] += amt
        rec[1].append((self.sems[sem], amt))
        return (sem, self.cnt[sem])

    def tok(self, sem):
        return (sem, self.cnt[sem])

    def selfwait(self, e):
        s = Plan.OWN[e]
        v = self.cnt[s]
        if v <= 0:
            return
        h = self.sems[s]
        self.q[e].append([lambda eng, h=h, v=v: eng.wait_ge(h, v), []])

    def wait(self, e, tok):
        if tok is None:
            return
        if isinstance(tok, (list, tuple)) and tok and not isinstance(tok[0], str):
            for t in tok:
                self.wait(e, t)
            return
        s, v = tok
        if v <= 0 or Plan.OWN.get(e) == s:
            return
        if self.lastw[e].get(s, 0) >= v:
            return
        self.lastw[e][s] = v
        h = self.sems[s]
        self.q[e].append([lambda eng, h=h, v=v: eng.wait_ge(h, v), []])


class WRing:
    def __init__(self, p, sbuf, wdram):
        self.p = p
        self.sb = sbuf
        self.wdram = wdram
        self.plan = []
        self.gi = -1
        self.off = GROUP
        self.load_tok = None
        self.last_use = None
        self.free_toks = []

    def _close(self):
        rec, tok = self.last_use
        if tok is None:
            tok = self.p.inc(rec, "wf")
        self.free_toks.append(tok)
        self.last_use = None

    def get(self, key):
        p = self.p
        if self.off == GROUP:
            if self.last_use is not None:
                self._close()
            self.gi += 1
            self.off = 0
            g = self.gi
            if g >= RDEPTH:
                p.wait("g", self.free_toks[g - RDEPTH])
            slot = g % RDEPTH
            dst = self.sb[:, slot * GROUP * 128:(slot + 1) * GROUP * 128]
            src = self.wdram[g % NG]
            rec = p.emit("g", lambda eng, d=dst, s=src: eng.dma_start(d, s))
            self.load_tok = p.inc(rec, "w%d" % slot, 16)
        slot = self.gi % RDEPTH
        c = self.off
        self.off += 1
        self.plan.append(key)
        base = (slot * GROUP + c) * 128
        return self.sb[:, base:base + 128]

    def used(self, rec, tok=None):
        self.last_use = (rec, tok)

    def finish(self):
        if self.last_use is not None:
            self._close()


def build():
    nc = bass.Bass("TRN2", target_bir_lowering=False)
    x_in_h = nc.dram_tensor("x_in", [SPC, D, T], F32, kind="ExternalInput")
    y_out_h = nc.dram_tensor("y_out", [SPC, D, T], F32, kind="ExternalOutput")
    wstr = nc.dram_tensor("wstr", [NG, 128, GROUP * 128], F32,
                          kind="ExternalInput").ap()
    d_ropeC = nc.dram_tensor("d_ropeC", [128, T], FP16, kind="ExternalInput").ap()
    d_ropeS = nc.dram_tensor("d_ropeS", [128, T], FP16, kind="ExternalInput").ap()
    d_masks_h = nc.dram_tensor("d_masks", [4, 128, 512], BF16, kind="ExternalInput")
    d_pswap = nc.dram_tensor("d_pswap", [128, 128], BF16, kind="ExternalInput").ap()
    d_identb = nc.dram_tensor("d_identb", [128, 128], BF16, kind="ExternalInput").ap()
    d_identf = nc.dram_tensor("d_identf", [128, 128], F32, kind="ExternalInput").ap()
    d_onesrowf = nc.dram_tensor("d_onesrowf", [1, 128], F32, kind="ExternalInput").ap()
    d_onesrow16 = nc.dram_tensor("d_onesrow16", [1, 128], BF16, kind="ExternalInput").ap()
    d_onescol16 = nc.dram_tensor("d_onescol16", [128, 1], BF16, kind="ExternalInput").ap()
    d_onescolf = nc.dram_tensor("d_onescolf", [128, 1], F32, kind="ExternalInput").ap()
    d_ones2d = nc.dram_tensor("d_ones2d", [128, 64], F32, kind="ExternalInput").ap()
    d_gw_h = nc.dram_tensor("d_gw", [8, 128, 8], F32, kind="ExternalInput")
    d_bqk = nc.dram_tensor("d_bqk", [128, 16], F32, kind="ExternalInput").ap()
    d_bv1r = nc.dram_tensor("d_bv1r", [1, 1024], BF16, kind="ExternalInput").ap()
    d_bp1 = nc.dram_tensor("d_bp1", [128, 8], F32, kind="ExternalInput").ap()
    d_b1c = nc.dram_tensor("d_b1c", [128, 32], F32, kind="ExternalInput").ap()
    d_b2c = nc.dram_tensor("d_b2c", [128, 8], F32, kind="ExternalInput").ap()
    d_sc = nc.dram_tensor("d_sc", [1, 2], F32, kind="ExternalInput").ap()

    LAYOUT_BYTES = 206 * 1024 + 512
    backing = nc.alloc_sbuf_tensor("layout_backing", [128, LAYOUT_BYTES],
                                   mybir.dt.uint8)
    BASE = nc.lookup_mloc(backing).addr
    off = [BASE]

    def sz(shape, dtype):
        n = 1
        for d in shape[1:]:
            n *= d
        return n * (2 if dtype in (BF16, FP16) else 4)

    def sbal(name, shape, dtype, pos):
        o = (pos[0] + 63) // 64 * 64
        h = nc.alloc_sbuf_tensor_at(name, list(shape), dtype, offset=o)
        pos[0] = o + sz(shape, dtype)
        return h

    def sb(name, shape, dtype):
        return sbal(name, shape, dtype, off)

    xT = sb("xT", [128, 8 * T], F32)
    xnT = sb("xnT", [128, 8 * T], BF16)
    OT = sb("OT", [128, 8 * T], BF16)
    wring = sb("wring", [128, RDEPTH * GROUP * 128], BF16)
    identb = sb("identb", [128, 128], BF16)
    identf = sb("identf", [128, 128], F32)
    pswap = sb("pswap", [128, 128], BF16)
    onesrowf = sb("onesrowf", [1, 128], F32)
    onesrow16 = sb("onesrow16", [1, 128], BF16)
    onescol16 = sb("onescol16", [128, 1], BF16)
    onescolf = sb("onescolf", [128, 1], F32)
    ones2d = sb("ones2d", [128, 64], F32)
    gw16 = sb("gw16", [128, 64], BF16)
    bqk = sb("bqk", [128, 16], F32)
    bp1 = sb("bp1", [128, 8], F32)
    b1c = sb("b1c", [128, 32], F32)
    b2c = sb("b2c", [128, 8], F32)
    scgain = sb("scgain", [1, 2], F32)
    gainB = sb("gainB", [128, 1], F32)
    mg_sb = sb("mg_sb", [128, 8], F32)
    kbar = sb("kbar", [128, 2], F32)
    kbar16 = sb("kbar16", [128, 2], BF16)
    mcolneg = sb("mcolneg", [128, 32], F32)
    madacc = sb("madacc", [128, 128], F32)
    madsum = sb("madsum", [128, 32], F32)
    kcol = sb("kcol", [128, 32], F32)
    mu01 = sb("mu01", [1, 8], F32)
    mu16 = sb("mu16", [1, 1], BF16)
    comb = sb("comb", [128, 128], F32)
    x2tmp = sb("x2tmp", [128, 1024], F32)
    lntmp = sb("lntmp", [128, 512], F32)
    lnrow = sb("lnrow", [1, 2 * 512], F32)

    region0 = (off[0] + 63) // 64 * 64
    REGION = 56 * 1024
    LIMIT = region0 + REGION
    assert LIMIT <= BASE + LAYOUT_BYTES, (BASE, LIMIT)

    roff = [region0]

    def rb(name, shape, dtype):
        h = sbal(name, shape, dtype, roff)
        assert roff[0] <= LIMIT, (name, roff[0])
        return h

    ropeC = rb("ropeC", [128, T], FP16)
    ropeS = rb("ropeS", [128, T], FP16)
    qT = rb("qT", [128, T], BF16)
    kT = rb("kT", [128, T], BF16)
    vP = rb("vP", [128, NTT * 130], BF16)
    scratch2 = rb("scratch2", [128, T], BF16)
    vtmp = scratch2
    ering = rb("ering", [128, 4 * 512], BF16)
    masksb = rb("masksb", [128, 2048], BF16)
    zring = rb("zring", [128, 2 * 512], F32)
    u2ring = rb("u2ring", [128, 2 * 512], F32)
    kb_sb = rb("kb_sb", [128, 2 * 512], F32)
    rbuf = rb("rbuf", [128, 512], F32)
    recB = rb("recB", [64, 512], F32)
    mrow16 = rb("mrow16", [1, 2 * 512], BF16)
    krow16 = rb("krow16", [1, 2 * 512], BF16)
    rnt = nc.alloc_sbuf_tensor_at("rnt", [128, 512], F32,
                                  offset=nc.lookup_mloc(mrow16).addr)
    rnt2 = nc.alloc_sbuf_tensor_at("rnt2", [128, 512], F32,
                                   offset=nc.lookup_mloc(krow16).addr)
    bvB = rb("bvB", [128, 128], BF16)
    bv1r = rb("bv1r", [1, 1024], BF16)

    roff[0] = region0
    hbuf = rb("hbuf", [128, 12 * 1024], BF16)
    combB = rb("combB", [128, T], BF16)
    sgt = rb("sgt", [128, 4 * 512], BF16)
    crow16 = rb("crow16", [1, T], BF16)

    roff[0] = region0
    hgrp = rb("hgrp", [128, 8 * 1024], BF16)
    lrows = rb("lrows", [1, 8 * 1024], BF16)
    arow = rb("arow", [1, T], BF16)
    crow4 = rb("crow4", [1, 4 * 1024], BF16)
    lrows4 = rb("lrows4", [1, 4096], BF16)

    import contextlib
    es = contextlib.ExitStack()
    p512 = [es.enter_context(nc.psum_tensor("p512_%d" % i, [128, 512], F32))
            for i in range(5)]
    pA = es.enter_context(nc.psum_tensor("pA", [128, 1024], F32))
    pbf = es.enter_context(nc.psum_tensor("pbf", [128, 512], BF16))
    sems = {s_: es.enter_context(nc.semaphore("sem_" + s_))
            for s_ in ("w0", "w1", "gw", "io", "xl", "tbl", "out",
                       "t", "v", "s", "wf")}

    p = Plan(sems)
    ring = WRing(p, wring, wstr)
    rnt_war = [None]

    def emt(e, fn, sem=None, waits=(), amt=1):
        for w in waits:
            p.wait(e, w)
        rec = p.emit(e, fn)
        if sem:
            return p.inc(rec, sem, amt), rec
        return None, rec

    def mm(out_ap, lhs_ap, rhs_ap, start, stop, tp=None, sem=None, waits=()):
        return emt("t", lambda eng, o=out_ap, l=lhs_ap, r=rhs_ap, a=start, b=stop,
                 t_=tp: eng.matmul(o, l, r, start=a, stop=b, tile_position=t_,
                                   skip_group_check=True),
                 sem=sem, waits=waits)

    def wmm(out_ap, key, rhs_ap, start, stop, tp=None, sem=None, waits=()):
        lhs = ring.get(key)
        tok, rec = mm(out_ap, lhs, rhs_ap, start, stop, tp, sem,
                      tuple(waits) + (ring.load_tok,))
        ring.used(rec, tok)
        return tok, rec

    def vop(fn, sem="v", waits=(), sw=False):
        p.selfwait("v")
        return emt("v", fn, sem=sem, waits=waits)

    def sop(fn, sem="s", waits=(), sw=False):
        p.selfwait("s")
        return emt("s", fn, sem=sem, waits=waits)

    def xn(c, t0, n):
        return xnT[:, T * c + t0: T * c + t0 + n]

    def xt(c, t0, n):
        return xT[:, T * c + t0: T * c + t0 + n]

    def ydma(dst, src, sem="io", waits=()):
        return emt("y", lambda eng, d=dst, s=src: eng.dma_start(d, s),
                   sem=sem, waits=waits, amt=16)

    for dst, src in ((identb[:, :], d_identb), (identf[:, :], d_identf),
                     (pswap[:, :], d_pswap), (onesrowf[:, :], d_onesrowf),
                     (onesrow16[:, :], d_onesrow16), (onescol16[:, :], d_onescol16),
                     (onescolf[:, :], d_onescolf), (ones2d[:, :], d_ones2d),
                     (bqk[:, :], d_bqk), (bp1[:, :], d_bp1),
                     (b1c[:, :], d_b1c), (b2c[:, :], d_b2c),
                     (scgain[:, :], d_sc)):
        ydma(dst, src)
    _, grec = emt("g", lambda eng: eng.dma_start(gw16[:, :], gw_src))
    gw_tok = p.inc(grec, "gw", 16)
    const_tok = p.tok("io")

    def xv(sq):
        return bass.AP(x_in_h, sq * D * T, [[T, 128], [128 * T, NCH], [1, T]])

    def yv(sq):
        return bass.AP(y_out_h, sq * D * T, [[T, 128], [128 * T, NCH], [1, T]])

    masks_src = bass.AP(d_masks_h, 0, [[512, 128], [128 * 512, 4], [1, 512]])
    gw_src = bass.AP(d_gw_h, 0, [[8, 128], [128 * 8, 8], [1, 8]])

    gB_t, _ = mm(p512[0][:, 0:1], onesrowf[:, :], scgain[:, 1:2], True, True,
                 sem="t", waits=(const_tok,))
    gB_v, _ = vop(lambda eng: eng.tensor_copy(gainB[:, :], p512[0][:, 0:1]),
                  waits=(gB_t,))

    st = {"x_ready": None, "x_readers": None, "x2": [None, None],
          "p512": [gB_v, None, None, None, None], "pA": None, "pbf": None}

    def phase_ln(layer):
        pm_tok = st["p512"][4]
        pq_tok = st["pA"]
        last = None
        for u in range(NSUP):
            pm = p512[4][0:1, :]
            pq = pA[0:1, 0:512]
            for c in range(NCH):
                tok, _ = mm(pm, onescolf[:, :], xt(c, 512 * u, 512),
                            c == 0, c == NCH - 1,
                            sem="t" if c == NCH - 1 else None,
                            waits=(st["x_ready"], pm_tok))
            sum_tok = tok
            for c in range(NCH):
                sl = (c % 2) * 512
                stok, _ = sop(lambda eng, c=c, u=u, sl=sl: eng.activation(
                    x2tmp[:, sl:sl + 512], xt(c, 512 * u, 512), AF.Square),
                    waits=(st["x_ready"], st["x2"][c % 2]))
                ttok, _ = mm(pq, onescolf[:, :], x2tmp[:, sl:sl + 512],
                             c == 0, c == NCH - 1,
                             sem="t", waits=(stok, pq_tok))
                st["x2"][c % 2] = ttok
            ssq_tok = ttok
            mrow = lnrow[0:1, 0:512]
            vrow = lnrow[0:1, 512:1024]
            rrow = vrow
            t1, _ = vop(lambda eng: eng.tensor_scalar_mul(mrow, pm, 1.0 / D),
                        waits=(sum_tok,))
            t2, _ = vop(lambda eng: eng.tensor_mul(vrow, mrow, mrow), sw=True)
            t3, _ = vop(lambda eng: eng.scalar_tensor_tensor(
                vrow, pq, 1.0 / D, vrow, ALU.mult, ALU.subtract),
                waits=(ssq_tok,))
            t3b, _ = vop(lambda eng: eng.tensor_scalar_add(vrow, vrow, 1e-5),
                         sw=True)
            t4, _ = sop(lambda eng: eng.activation(rrow, vrow, AF.Sqrt),
                        waits=(t3b,))
            t5, _ = vop(lambda eng: eng.reciprocal(rrow, rrow),
                        waits=(t4,), sw=True)
            tb1, _ = mm(p512[4][:, :], onesrowf[:, :], mrow, True, True,
                        sem="t", waits=(t1, t5))
            tb2, _ = mm(pA[:, 0:512], onesrowf[:, :], rrow, True, True,
                        sem="t", waits=(t5,))
            for c in range(NCH):
                tv1, _ = vop(lambda eng, c=c, u=u: eng.tensor_sub(
                    lntmp[:, :], xt(c, 512 * u, 512), p512[4][:, :]),
                    waits=(tb1,))
                tv2, _ = vop(lambda eng, c=c, u=u: eng.tensor_mul(
                    xn(c, 512 * u, 512), lntmp[:, :], pA[:, 0:512]),
                    waits=(tb2,))
            pm_tok = tv1
            pq_tok = tv2
            last = tv2
        st["p512"][4] = pm_tok
        st["pA"] = pq_tok
        return last

    def phase_attention(layer, xn_tok):
        L0 = layer == 0
        ot_last = None
        sc2_tok = None
        allw = (p.tok("t"), p.tok("v"), p.tok("s"))
        ydma(masksb[:, :], masks_src, sem="tbl", waits=allw)
        if L0:
            ydma(ropeC[:, :], d_ropeC, sem="tbl")
            ydma(ropeS[:, :], d_ropeS, sem="tbl")
        else:
            ydma(bv1r[:, :], d_bv1r, sem="tbl")
        mask_tok = rope_tok = bv_tok = p.tok("tbl")
        for pair in range(CFG["pairs"]):
            names = ("wq", "wk", "wv") if L0 else ("q1", "k1", "v1")
            done = {}
            for wi, wnm in enumerate(names):
                dstbuf = (qT, kT, vtmp)[wi]
                utoks = [None] * NSUP
                for k in range(NCH):
                    lhs = ring.get((wnm, (pair, k)))
                    ltok = ring.load_tok
                    lrec = None
                    for u in range(NSUP):
                        tok, lrec = mm(p512[u][:, :], lhs,
                                       xn(k, 512 * u, 512), k == 0, k == NCH - 1,
                                       sem="t" if k == NCH - 1 else None,
                                       waits=(xn_tok, ltok, st["p512"][u]))
                        if k == NCH - 1:
                            utoks[u] = tok
                    ring.used(lrec, tok if tok else None)
                etok = None
                for u in range(NSUP):
                    if (not L0) and wi < 2:
                        bcol = bqk[:, 8 * wi + pair:8 * wi + pair + 1]
                        etok, _ = vop(lambda eng, d=dstbuf, u=u, bc=bcol:
                                      eng.tensor_scalar_add(
                                          d[:, 512 * u:512 * u + 512],
                                          p512[u][:, :], bc),
                                      waits=(utoks[u], const_tok))
                    else:
                        etok, _ = vop(lambda eng, d=dstbuf, u=u:
                                      eng.tensor_copy(
                                          d[:, 512 * u:512 * u + 512],
                                          p512[u][:, :]),
                                      waits=(utoks[u],))
                    st["p512"][u] = etok
                done[wnm] = etok
            q_done = done[names[0]]
            k_done = done[names[1]]
            v_done = done[names[2]]

            if L0:
                newdone = []
                for buf, dtok in ((qT, q_done), (kT, k_done)):
                    t4 = None
                    for u in range(NSUP):
                        t1, _ = mm(p512[4][:, :], pswap[:, :],
                                   buf[:, 512 * u:512 * u + 512], True, True,
                                   sem="t", waits=(dtok, const_tok, rope_tok,
                                                   st["p512"][4]))
                        t2, _ = vop(lambda eng, u=u: eng.tensor_mul(
                            zring[:, 0:512], p512[4][:, :],
                            ropeS[:, 512 * u:512 * u + 512]),
                            waits=(t1, rope_tok))
                        st["p512"][4] = t2
                        t3, _ = vop(lambda eng, buf=buf, u=u: eng.tensor_mul(
                            zring[:, 512:1024], buf[:, 512 * u:512 * u + 512],
                            ropeC[:, 512 * u:512 * u + 512]))
                        t4, _ = vop(lambda eng, buf=buf, u=u: eng.tensor_add(
                            buf[:, 512 * u:512 * u + 512], zring[:, 0:512],
                            zring[:, 512:1024]))
                    newdone.append(t4)
                q_done, k_done = newdone

            mz, _ = vop(lambda eng: eng.memset(
                bass.AP(vP, 64, [[130 * NTT, 128], [65, 2 * NTT]]), 1.0))
            tbv2 = None
            if not L0:
                tbv, _ = mm(p512[4][:, 0:128], onesrow16[:, :],
                            bv1r[0:1, 128 * pair:128 * pair + 128], True, True,
                            sem="t", waits=(const_tok, bv_tok, st["p512"][4]))
                tbv2, _ = vop(lambda eng: eng.tensor_copy(
                    bvB[:, :], p512[4][:, 0:128]), waits=(tbv,))
                st["p512"][4] = tbv2
            vp_last = None
            for tt in range(NTT):
                sl = (tt % 4) * 128
                ttr, _ = emt("t", lambda eng, tt=tt, sl=sl: eng.transpose(
                    pbf[:, sl:sl + 128], vtmp[:, 128 * tt:128 * tt + 128],
                    identb[:, :]), sem="t",
                    waits=(v_done, const_tok, st["pbf"]))
                dst = bass.AP(vP, 130 * tt, [[130 * NTT, 128], [65, 2], [1, 64]])
                srcap = bass.AP(pbf, sl, [[512, 128], [64, 2], [1, 64]])
                if L0:
                    tv, _ = vop(lambda eng, d=dst, s=srcap: eng.tensor_copy(d, s),
                                waits=(ttr, mz))
                else:
                    bap = bass.AP(bvB, 0, [[128, 128], [64, 2], [1, 64]])
                    tv, _ = vop(lambda eng, d=dst, s=srcap, b=bap:
                                eng.tensor_tensor(d, s, b, ALU.add),
                                waits=(ttr, mz, tbv2))
                st["pbf"] = tv
                vp_last = tv
            vp_done = vp_last

            if L0:
                for qi, (buf, dtok) in enumerate(((qT, q_done), (kT, k_done))):
                    sqk, _ = sop(lambda eng, buf=buf: eng.activation(
                        scratch2[:, :], buf[:, :], AF.Square),
                        waits=(dtok, sc2_tok, vp_done))
                    for u in range(NSUP):
                        tn, _ = mm(pA[0:1, 512:1024], onescol16[:, :],
                                   scratch2[:, 512 * u:512 * u + 512], True, True,
                                   sem="t", waits=(sqk, st["pA"]))
                        tm, _ = vop(lambda eng, u=u: eng.reduce_max(
                            mu01[0:1, u:u + 1], pA[0:1, 512:1024], AX.X),
                            waits=(tn,))
                        st["pA"] = tm
                    tmx, _ = vop(lambda eng, qi=qi: eng.reduce_max(
                        mu01[0:1, 4 + qi:5 + qi], mu01[0:1, 0:4], AX.X),
                        sw=True)
                    sc2_tok = p.tok("t")
                tadd, _ = vop(lambda eng: eng.tensor_mul(
                    mu01[0:1, 6:7], mu01[0:1, 4:5], mu01[0:1, 5:6]))
                tsql, _ = sop(lambda eng: eng.activation(
                    mu01[0:1, 7:8], mu01[0:1, 6:7], AF.Ln), waits=(tadd,))
                tsq, _ = sop(lambda eng: eng.activation(
                    mu01[0:1, 7:8], mu01[0:1, 7:8], AF.Exp, scale=0.5),
                    sw=True)
                tsc, _ = vop(lambda eng: eng.tensor_scalar_mul(
                    mu01[0:1, 6:7], mu01[0:1, 7:8], -1.0 / 8.0), waits=(tsq,))
                t16, _ = vop(lambda eng: eng.tensor_copy(mu16[:, :],
                                                         mu01[0:1, 6:7]))
                tmg, _ = mm(p512[4][:, 0:1], onesrow16[:, :], mu16[:, :],
                            True, True, sem="t", waits=(t16, st["p512"][4]))
                tmg2, _ = vop(lambda eng, pair=pair: eng.tensor_copy(
                    mg_sb[:, pair:pair + 1], p512[4][:, 0:1]), waits=(tmg,))
                st["p512"][4] = tmg2
                stats_tok = tmg2
            else:
                tkb, _ = vop(lambda eng: eng.reduce_sum(
                    kbar[:, 0:1], kT[:, :], AX.X), waits=(k_done,))
                tkb2, _ = vop(lambda eng: eng.tensor_copy(
                    kbar16[:, 0:1], kbar[:, 0:1]), sw=True)
                for hh in range(2):
                    for tt in range(NTT):
                        tmc, _ = mm(p512[4][:, tt:tt + 1],
                                    qT[64 * hh:64 * hh + 64,
                                       128 * tt:128 * tt + 128],
                                    kbar16[64 * hh:64 * hh + 64, 0:1],
                                    True, True,
                                    sem="t" if tt == NTT - 1 else None,
                                    waits=(tkb2, q_done, st["p512"][4]))
                    tmc2, _ = vop(lambda eng, hh=hh: eng.tensor_scalar_mul(
                        mcolneg[:, 16 * hh:16 * hh + 16], p512[4][:, 0:NTT],
                        -1.0 / T), waits=(tmc,))
                    st["p512"][4] = tmc2
                mstat_tok = tmc2
                half_ring = [pA[:, 0:512], pA[:, 512:1024]] + \
                    [p512[i][:, :] for i in range(4)]
                hr_tok = [st["pA"], st["pA"]] + [st["p512"][i] for i in range(4)]
                hi_ = 0
                for tt in range(NTT):
                    for hh in range(2):
                        for half in range(4):
                            slot = hi_ % 6
                            ps = half_ring[slot]
                            tsm, _ = mm(ps, qT[64 * hh:64 * hh + 64,
                                               128 * tt:128 * tt + 128],
                                        kT[64 * hh:64 * hh + 64,
                                           512 * half:512 * half + 512],
                                        True, True, tp=(64 * hh, 0), sem="t",
                                        waits=(hr_tok[slot], q_done, k_done,
                                               mstat_tok))
                            acol = madacc[:, 64 * hh + 4 * tt + half:
                                          64 * hh + 4 * tt + half + 1]
                            bcol = mcolneg[:, 16 * hh + tt:16 * hh + tt + 1]
                            tab, _ = sop(lambda eng, ps=ps, a=acol, bc=bcol,
                                         half=half, hh=hh: eng.activation(
                                             scratch2[:, 1024 * hh +
                                                      512 * (half % 2):
                                                      1024 * hh +
                                                      512 * (half % 2) + 512],
                                             ps, AF.Abs, bias=bc,
                                             accum_out=a),
                                         waits=(tsm,))
                            hr_tok[slot] = tab
                            hi_ += 1
                st["pA"] = [hr_tok[0], hr_tok[1]]
                for i in range(4):
                    st["p512"][i] = hr_tok[2 + i]
                mad_tail = [hr_tok[i] for i in range(6)]
                tms = []
                for hh in range(2):
                    aps = [bass.AP(madacc, 64 * hh + q_, [[128, 128], [4, 16]])
                           for q_ in range(4)]
                    ms = madsum[:, 16 * hh:16 * hh + 16]
                    tq1, _ = vop(lambda eng, a=aps, ms=ms: eng.tensor_add(
                        ms, a[0], a[1]), waits=tuple(mad_tail))
                    tq2, _ = vop(lambda eng, a=aps: eng.tensor_add(
                        x2tmp[:, 0:16], a[2], a[3]))
                    tq3, _ = vop(lambda eng, ms=ms: eng.tensor_add(
                        ms, ms, x2tmp[:, 0:16]), sw=True)
                    tms.append(tq3)
                tk1, _ = vop(lambda eng: eng.tensor_scalar(
                    kcol[:, :], madsum[:, :], 1.0 / T, 1e-6, ALU.mult, ALU.add),
                    waits=tuple(tms), sw=True)
                tk2, _ = vop(lambda eng: eng.reciprocal(
                    kcol[:, :], kcol[:, :]), sw=True)
                tk3, _ = vop(lambda eng: eng.tensor_scalar_mul(
                    kcol[:, :], kcol[:, :], gainB[:, 0:1]), waits=(gB_v,),
                    sw=True)
                stats_tok = tk3

            ztok = [None, None]
            u2tok = [None, None]
            etok_slot = [None, None, None, None]
            for u in range(NSUP):
                ntc = 4 * (u + 1)
                kb_toks = [None, None]
                mrow_toks = [None, None]
                if not L0:
                    for hh in range(2):
                        tmr, _ = mm(p512[4][0:1, :],
                                    kbar16[64 * hh:64 * hh + 64, 0:1],
                                    qT[64 * hh:64 * hh + 64,
                                       512 * u:512 * u + 512],
                                    True, True, sem="t",
                                    waits=(tkb2, q_done, st["p512"][4]))
                        tmr2, _ = vop(lambda eng, hh=hh:
                                      eng.tensor_scalar_mul(
                                          mrow16[0:1, 512 * hh:512 * hh + 512],
                                          p512[4][0:1, :], -1.0 / T),
                                      waits=(tmr,))
                        st["p512"][4] = tmr2
                        mrow_toks[hh] = tmr2
                        ttr2 = None
                        for j in range(4):
                            tt_ = 4 * u + j
                            ttr2, _ = emt("t", lambda eng, hh=hh, tt_=tt_, j=j:
                                        eng.transpose(
                                            pA[0:1, 512 + 128 * j:
                                               512 + 128 * j + 128],
                                            kcol[:, 16 * hh + tt_:
                                                 16 * hh + tt_ + 1],
                                            identf[:, :]),
                                        sem="t", waits=(stats_tok, st["pA"]))
                        tkr, _ = vop(lambda eng, hh=hh: eng.tensor_copy(
                            krow16[0:1, 512 * hh:512 * hh + 512],
                            pA[0:1, 512:1024]), waits=(ttr2,))
                        st["pA"] = tkr
                        pk_ap = pA[:, 512:1024] if hh == 1 else p512[4][:, :]
                        pk_tok = st["pA"] if hh == 1 else st["p512"][4]
                        tkb_, _ = mm(pk_ap, onesrow16[:, :],
                                     krow16[0:1, 512 * hh:512 * hh + 512],
                                     True, True, sem="t",
                                     waits=(tkr, pk_tok))
                        tkb2_, _ = vop(lambda eng, hh=hh, a=pk_ap:
                                       eng.tensor_copy(
                                           kb_sb[:, 512 * hh:512 * hh + 512], a),
                                       waits=(tkb_,))
                        if hh == 1:
                            st["pA"] = tkb2_
                        else:
                            st["p512"][4] = tkb2_
                        kb_toks[hh] = tkb2_
                pv_ps = [p512[4][0:65, :], pA[0:65, 0:512]]
                pv_ps_wait = [st["p512"][4], st["pA"]]
                pv_started = [False, False]
                pv_tok = [None, None]
                pend = []

                def do_pv(hh, tc, e_ap, e_tok):
                    tokpv, _ = mm(pv_ps[hh],
                                  vP[:, 130 * tc + 65 * hh:
                                     130 * tc + 65 * hh + 65],
                                  e_ap, not pv_started[hh], tc == ntc - 1,
                                  sem="t",
                                  waits=(e_tok, vp_done,
                                         pv_ps_wait[hh] if not pv_started[hh]
                                         else None))
                    pv_started[hh] = True
                    pv_tok[hh] = tokpv
                    return tokpv

                for tc in range(ntc):
                    for hh in range(2):
                        sring = tc % 2 + 2 * hh
                        ps = p512[sring]
                        tsm, _ = mm(ps[:, :],
                                    kT[64 * hh:64 * hh + 64,
                                       128 * tc:128 * tc + 128],
                                    qT[64 * hh:64 * hh + 64,
                                       512 * u:512 * u + 512],
                                    True, L0, tp=(64 * hh, 0),
                                    sem="t" if L0 else None,
                                    waits=(q_done, k_done, stats_tok,
                                           st["p512"][sring]))
                        if not L0:
                            tsm, _ = mm(ps[:, :], onesrow16[:, :],
                                        mrow16[0:1, 512 * hh:512 * hh + 512],
                                        False, True, sem="t",
                                        waits=(mrow_toks[hh],))
                        esl = (2 * tc + hh) % 4
                        e_ap = ering[:, 512 * esl:512 * esl + 512]
                        dmask = tc - 4 * u
                        if L0:
                            ertok, _ = sop(lambda eng, ps=ps, e=e_ap, pair=pair:
                                           eng.activation(
                                               e, ps[:, :], AF.Exp,
                                               bias=mg_sb[:, pair:pair + 1],
                                               scale=0.125),
                                           waits=(tsm, stats_tok,
                                                  etok_slot[esl]))
                        else:
                            zq = hh
                            tz, _ = vop(lambda eng, ps=ps, zq=zq, hh=hh:
                                        eng.tensor_mul(
                                            zring[:, 512 * zq:512 * zq + 512],
                                            ps[:, :],
                                            kb_sb[:, 512 * hh:512 * hh + 512]),
                                        waits=(tsm, kb_toks[hh], ztok[zq]))
                            tu, _ = vop(lambda eng, zq=zq: eng.tensor_scalar_max(
                                zring[:, 512 * zq:512 * zq + 512],
                                zring[:, 512 * zq:512 * zq + 512], 0.0))
                            t2_, _ = sop(lambda eng, zq=zq: eng.activation(
                                u2ring[:, 512 * zq:512 * zq + 512],
                                zring[:, 512 * zq:512 * zq + 512], AF.Square),
                                waits=(tu, u2tok[zq]))
                            ztok[zq] = t2_
                            ertok, _ = sop(lambda eng, zq=zq, e=e_ap:
                                           eng.activation(
                                               e, u2ring[:, 512 * zq:
                                                         512 * zq + 512],
                                               AF.Square),
                                           waits=(etok_slot[esl],))
                            u2tok[zq] = ertok
                        st["p512"][sring] = ertok
                        if 0 <= dmask <= 3:
                            ertok, _ = vop(lambda eng, e=e_ap, dmask=dmask:
                                           eng.tensor_mul(
                                               e, e,
                                               masksb[:, 512 * dmask:
                                                      512 * dmask + 512]),
                                           waits=(ertok, mask_tok))
                        pend.append((hh, tc, e_ap, ertok))
                    while len(pend) > 2:
                        hh_, tc_, ea_, et_ = pend.pop(0)
                        tpv = do_pv(hh_, tc_, ea_, et_)
                        etok_slot[(2 * tc_ + hh_) % 4] = tpv
                for hh_, tc_, ea_, et_ in pend:
                    tpv = do_pv(hh_, tc_, ea_, et_)
                    etok_slot[(2 * tc_ + hh_) % 4] = tpv
                for hh in range(2):
                    pvp = pv_ps[hh]
                    if L0:
                        tr1, _ = vop(lambda eng, pvp=pvp: eng.reciprocal(
                            rbuf[64:65, :], pvp[64:65, :]),
                            waits=(pv_tok[hh],))
                    else:
                        war = (rnt_war[0],) if rnt_war[0] is not None else ()
                        tr0, _ = vop(lambda eng, pvp=pvp: eng.tensor_scalar_add(
                            rbuf[64:65, :], pvp[64:65, :], 1e-6),
                            waits=(pv_tok[hh],))
                        trl, _ = sop(lambda eng: eng.activation(
                            rnt[64:65, :], rbuf[64:65, :], AF.Ln),
                            waits=(tr0,) + war)
                        rho = rbuf[64:65, :]
                        tre, _ = sop(lambda eng: eng.activation(
                            rnt[64:65, :], rnt[64:65, :], AF.Exp, scale=-1.0),
                            sw=True)
                        tt1, _ = vop(lambda eng, rho=rho: eng.tensor_mul(
                            rnt2[64:65, :], rho, rnt[64:65, :]),
                            waits=(tre,), sw=True)
                        tt2, _ = vop(lambda eng: eng.tensor_scalar(
                            rnt2[64:65, :], rnt2[64:65, :], -1.0, 2.0,
                            ALU.mult, ALU.add), sw=True)
                        tt3, _ = vop(lambda eng: eng.tensor_mul(
                            rnt[64:65, :], rnt[64:65, :], rnt2[64:65, :]),
                            sw=True)
                        tt4, _ = vop(lambda eng, rho=rho: eng.tensor_mul(
                            rnt2[64:65, :], rho, rnt[64:65, :]), sw=True)
                        tt5, _ = vop(lambda eng: eng.tensor_scalar(
                            rnt2[64:65, :], rnt2[64:65, :], -1.0, 2.0,
                            ALU.mult, ALU.add), sw=True)
                        tr1, _ = vop(lambda eng: eng.tensor_mul(
                            rbuf[64:65, :], rnt[64:65, :], rnt2[64:65, :]),
                            sw=True)
                        rnt_war[0] = tr1
                    trb, _ = mm(pA[0:64, 512:1024], ones2d[64:65, 0:64],
                                rbuf[64:65, :], True, True, sem="t",
                                waits=(tr1, st["pA"]))
                    tre, _ = vop(lambda eng: eng.tensor_copy(
                        recB[:, :], pA[0:64, 512:1024]), waits=(trb,))
                    to_, _ = vop(lambda eng, pvp=pvp, hh=hh, u=u, pair=pair:
                                 eng.tensor_mul(
                                     OT[64 * hh:64 * hh + 64,
                                        T * pair + 512 * u:
                                        T * pair + 512 * u + 512],
                                     pvp[0:64, :], recB[:, :]), sw=True)
                    if hh == 0:
                        st["p512"][4] = to_
                        st["pA"] = tre
                    else:
                        st["pA"] = [tre, to_]
                    ot_last = to_
        return ot_last

    def phase_outproj(layer, ot_tok):
        L0 = layer == 0
        nm = "wo" if L0 else "wp"
        last = None
        for f in range(8):
            for pk in range(8):
                lhs = ring.get((nm, (f, pk)))
                ltok = ring.load_tok
                lrec = None
                for u in range(NSUP):
                    ps = p512[u]
                    tok, lrec = mm(ps[:, :], lhs,
                                   OT[:, T * pk + 512 * u:T * pk + 512 * u + 512],
                                   pk == 0, pk == 7,
                                   sem="t" if pk == 7 else None,
                                   waits=(ot_tok, ltok, st["p512"][u]))
                    if pk == 7:
                        if L0:
                            ta, _ = vop(lambda eng, f=f, u=u, ps=ps:
                                        eng.tensor_add(xt(f, 512 * u, 512),
                                                       xt(f, 512 * u, 512),
                                                       ps[:, :]), waits=(tok,))
                        else:
                            ta, _ = vop(lambda eng, f=f, u=u, ps=ps:
                                        eng.scalar_tensor_tensor(
                                            xt(f, 512 * u, 512), ps[:, :],
                                            bp1[:, f:f + 1], xt(f, 512 * u, 512),
                                            ALU.add, ALU.add),
                                        waits=(tok, const_tok))
                        st["p512"][u] = ta
                        last = ta
                ring.used(lrec, tok)
        return last

    def phase_moe(xn_tok):
        comb_tok = None
        for tt in range(NTT):
            ps = p512[tt % 2]
            for c in range(NCH):
                tok, _ = mm(ps[:, 0:8], xn(c, 128 * tt, 128),
                            gw16[:, 8 * c:8 * c + 8], c == 0, c == NCH - 1,
                            sem="t" if c == NCH - 1 else None,
                            waits=(xn_tok, gw_tok, st["p512"][tt % 2]))
            lt = x2tmp[:, 0:8]
            m1 = x2tmp[:, 8:9]
            m2 = x2tmp[:, 9:10]
            nm1 = x2tmp[:, 10:11]
            w2c = x2tmp[:, 11:12]
            w1c = x2tmp[:, 12:13]
            eq1 = x2tmp[:, 16:24]
            msk = x2tmp[:, 24:32]
            eq2 = x2tmp[:, 32:40]
            t0, _ = vop(lambda eng, ps=ps, lt=lt: eng.tensor_copy(lt, ps[:, 0:8]),
                        waits=(tok,))
            st["p512"][tt % 2] = t0
            vop(lambda eng, lt=lt, m1=m1: eng.reduce_max(m1, lt, AX.X), sw=True)
            vop(lambda eng, lt=lt, m1=m1, eq1=eq1: eng.tensor_scalar(
                eq1, lt, m1, None, ALU.is_equal), sw=True)
            vop(lambda eng, lt=lt, eq1=eq1, msk=msk: eng.scalar_tensor_tensor(
                msk, eq1, -1e30, lt, ALU.mult, ALU.add), sw=True)
            t4, _ = vop(lambda eng, msk=msk, m2=m2: eng.reduce_max(m2, msk, AX.X),
                        sw=True)
            t4b, _ = vop(lambda eng, m1=m1, nm1=nm1: eng.tensor_scalar_mul(
                nm1, m1, -1.0), sw=True)
            t5, _ = sop(lambda eng, m2=m2, nm1=nm1, w2c=w2c: eng.activation(
                w2c, m2, AF.Sigmoid, bias=nm1, scale=1.0), waits=(t4, t4b))
            t6, _ = vop(lambda eng, w2c=w2c, w1c=w1c: eng.tensor_scalar(
                w1c, w2c, -1.0, 1.0, ALU.mult, ALU.add), waits=(t5,))
            vop(lambda eng, msk=msk, m2=m2, eq2=eq2: eng.tensor_scalar(
                eq2, msk, m2, None, ALU.is_equal), sw=True)
            vop(lambda eng, eq1=eq1, w1c=w1c, tt=tt: eng.tensor_scalar_mul(
                comb[:, 8 * tt:8 * tt + 8], eq1, w1c), sw=True)
            comb_tok, _ = vop(lambda eng, w2c=w2c, eq2=eq2, tt=tt:
                              eng.scalar_tensor_tensor(
                                  comb[:, 8 * tt:8 * tt + 8], eq2, w2c,
                                  comb[:, 8 * tt:8 * tt + 8],
                                  ALU.mult, ALU.add), sw=True)

        h_last_reader = None
        xadd = None
        for e in range(E + 1):
            shared = e == E
            cb_tok = None
            if not shared:
                crt = None
                for tt in range(NTT):
                    ttr, _ = emt("t", lambda eng, tt=tt, e=e: eng.transpose(
                        pA[0:1, 128 * (tt % 4):128 * (tt % 4) + 128],
                        comb[:, 8 * tt + e:8 * tt + e + 1], identf[:, :]),
                        sem="t", waits=(comb_tok, const_tok, st["pA"]))
                    if tt % 4 == 3:
                        crt, _ = vop(lambda eng, tt=tt: eng.tensor_copy(
                            crow16[0:1, 128 * (tt - 3):128 * (tt - 3) + 512],
                            pA[0:1, 0:512]), waits=(ttr,))
                        st["pA"] = crt
                for u in range(NSUP):
                    tcb, _ = mm(pA[:, 0:512], onesrow16[:, :],
                                crow16[0:1, 512 * u:512 * u + 512], True, True,
                                sem="t", waits=(crt, st["pA"]))
                    cb_tok, _ = vop(lambda eng, u=u: eng.tensor_copy(
                        combB[:, 512 * u:512 * u + 512], pA[:, 0:512]),
                        waits=(tcb,))
                    st["pA"] = cb_tok
            wg_nm, wu_nm, wd_nm = ("sg", "su", "sd") if shared else \
                ("wg", "wu", "wd")
            for half in range(2):
                uoff = 2 * half
                sg_tok = [None] * 4
                hw_tok = {}
                for f0 in range(0, 12, 2):
                    gstop = {}
                    for fi, f in enumerate((f0, f0 + 1)):
                        bs = 2 * fi
                        for k in range(NCH):
                            key = (wg_nm, (f, k)) if shared else \
                                (wg_nm, (e, f, k))
                            lhs = ring.get(key)
                            ltok = ring.load_tok
                            lrec = None
                            for ui in range(2):
                                tok, lrec = mm(
                                    p512[bs + ui][:, :], lhs,
                                    xn(k, 512 * (uoff + ui), 512),
                                    k == 0, k == NCH - 1,
                                    sem="t" if k == NCH - 1 else None,
                                    waits=(xn_tok, ltok,
                                           st["p512"][bs + ui]))
                                if k == NCH - 1:
                                    gstop[(fi, ui)] = tok
                            ring.used(lrec, tok if k == NCH - 1 else None)
                        for ui in range(2):
                            bk = bs + ui
                            ts_, _ = sop(lambda eng, bk=bk, sl=512 * bk:
                                         eng.activation(
                                             sgt[:, sl:sl + 512],
                                             p512[bk][:, :], AF.Silu),
                                         waits=(gstop[(fi, ui)], sg_tok[bk]))
                            st["p512"][bk] = ts_
                            hw_tok[(fi, ui, "silu")] = ts_
                    for fi, f in enumerate((f0, f0 + 1)):
                        bs = 2 * fi
                        ustop = {}
                        for k in range(NCH):
                            key = (wu_nm, (f, k)) if shared else \
                                (wu_nm, (e, f, k))
                            lhs = ring.get(key)
                            ltok = ring.load_tok
                            lrec = None
                            for ui in range(2):
                                tok, lrec = mm(
                                    p512[bs + ui][:, :], lhs,
                                    xn(k, 512 * (uoff + ui), 512),
                                    k == 0, k == NCH - 1,
                                    sem="t" if k == NCH - 1 else None,
                                    waits=(hw_tok[(fi, ui, "silu")]
                                           if k == 0 else None, ltok))
                                if k == NCH - 1:
                                    ustop[ui] = tok
                            ring.used(lrec, tok if k == NCH - 1 else None)
                        for ui in range(2):
                            bk = bs + ui
                            hsl = hbuf[:, 1024 * f + 512 * ui:
                                       1024 * f + 512 * ui + 512]
                            if shared:
                                th, _ = vop(lambda eng, bk=bk, hsl=hsl,
                                            sl=512 * bk: eng.tensor_mul(
                                                hsl, sgt[:, sl:sl + 512],
                                                p512[bk][:, :]),
                                            waits=(ustop[ui], h_last_reader))
                            else:
                                vop(lambda eng, sl=512 * bk, u=uoff + ui:
                                    eng.tensor_mul(
                                        x2tmp[:, 0:512], sgt[:, sl:sl + 512],
                                        combB[:, 512 * u:512 * u + 512]),
                                    waits=(hw_tok[(fi, ui, "silu")], cb_tok,
                                           st["x2"][0]))
                                th, _ = vop(lambda eng, bk=bk, hsl=hsl:
                                            eng.tensor_mul(
                                                hsl, x2tmp[:, 0:512],
                                                p512[bk][:, :]),
                                            waits=(ustop[ui], h_last_reader))
                            st["x2"][0] = th
                            sg_tok[bk] = th
                            st["p512"][bk] = th
                h_tok = th
                for o in range(8):
                    bs = 0 if o % 2 == 0 else 2
                    dstop = [None, None]
                    for j in range(12):
                        key = (wd_nm, (o, j)) if shared else \
                            (wd_nm, (e, o, j))
                        lhs = ring.get(key)
                        ltok = ring.load_tok
                        lrec = None
                        for ui in range(2):
                            td, lrec = mm(
                                p512[bs + ui][:, :], lhs,
                                hbuf[:, 1024 * j + 512 * ui:
                                     1024 * j + 512 * ui + 512],
                                j == 0, j == 11,
                                sem="t" if j == 11 else None,
                                waits=(h_tok, ltok, st["p512"][bs + ui]))
                            if j == 11:
                                dstop[ui] = td
                        ring.used(lrec, td)
                    h_last_reader = dstop[1]
                    for ui in range(2):
                        u = uoff + ui
                        ta2, _ = vop(lambda eng, o=o, u=u, bk=bs + ui:
                                     eng.tensor_add(
                                         xt(o, 512 * u, 512),
                                         xt(o, 512 * u, 512),
                                         p512[bk][:, :]),
                                     waits=(dstop[ui],))
                        st["p512"][bs + ui] = ta2
                        xadd = ta2
        return xadd

    def phase_ffn1(xn_tok):
        lr_tok = None
        for i in range(8):
            tokidx = 1 + 2 * i
            for half in range(2):
                ttr = None
                for cc in range(4):
                    c = 4 * half + cc
                    ttr, _ = emt("t", lambda eng, c=c, cc=cc, tokidx=tokidx:
                               eng.transpose(
                                   pbf[0:1, 128 * cc:128 * cc + 128],
                                   xnT[:, T * c + tokidx:T * c + tokidx + 1],
                                   identb[:, :]),
                               sem="t", waits=(xn_tok, const_tok, st["pbf"]))
                lr_tok, _ = vop(lambda eng, i=i, half=half: eng.tensor_copy(
                    lrows[0:1, 1024 * i + 512 * half:1024 * i + 512 * half + 512],
                    pbf[0:1, 0:512]), waits=(ttr,))
                st["pbf"] = lr_tok
        l4_tok = None
        for i4 in range(4):
            for q in range(4):
                l4_tok, _ = vop(lambda eng, i4=i4, q=q: eng.tensor_copy(
                    lrows4[0:1, 1024 * i4 + 256 * q:1024 * i4 + 256 * q + 256],
                    bass.AP(lrows, 1024 * (4 + i4) + q, [[8192, 1], [4, 256]])),
                    waits=(lr_tok,))
        xadd = None
        hgrp_reader = None
        dump_tok = None
        for g in range(4):
            odd_col = 2 + 2 * g
            a_tok = None
            for u in range(NSUP):
                tam = None
                for c in range(NCH):
                    tam, _ = mm(p512[4][0:1, :],
                                bass.AP(xnT, T * c + odd_col,
                                        [[8 * T, 128], [1, 1]]),
                                xn(c, 512 * u, 512), c == 0, c == NCH - 1,
                                sem="t" if c == NCH - 1 else None,
                                waits=(xn_tok, st["p512"][4]))
                a_tok, _ = vop(lambda eng, u=u: eng.tensor_scalar_mul(
                    arow[0:1, 512 * u:512 * u + 512], p512[4][0:1, :],
                    scgain[0:1, 0:1]), waits=(tam, const_tok))
                st["p512"][4] = a_tok
            c_tok = [None, None]
            for half in range(2):
                uoff = 2 * half
                h_tok = None
                for f0 in range(0, 8, 2):
                    for fi, f in enumerate((f0, f0 + 1)):
                        bs = 2 * fi
                        hstop = {}
                        for k in range(NCH):
                            lhs = ring.get(("w1", (g, f, k)))
                            ltok = ring.load_tok
                            lrec = None
                            for ui in range(2):
                                tok, lrec = mm(
                                    p512[bs + ui][:, :], lhs,
                                    xn(k, 512 * (uoff + ui), 512),
                                    k == 0, False,
                                    waits=(xn_tok, ltok, st["p512"][bs + ui],
                                           hgrp_reader if k == 0 else None))
                            ring.used(lrec, tok)
                        for ui in range(2):
                            tl, _ = mm(p512[bs + ui][:, :],
                                       lrows[0:1, 1024 * g + 128 * f:
                                             1024 * g + 128 * f + 128],
                                       arow[0:1, 512 * (uoff + ui):
                                            512 * (uoff + ui) + 512],
                                       False, True, sem="t",
                                       waits=(lr_tok, a_tok))
                            hstop[ui] = tl
                        for ui in range(2):
                            h_tok, _ = sop(
                                lambda eng, bk=bs + ui, g=g, f=f, ui=ui:
                                eng.activation(
                                    hgrp[:, 1024 * f + 512 * ui:
                                         1024 * f + 512 * ui + 512],
                                    p512[bk][:, :], AF.Gelu,
                                    bias=b1c[:, 8 * g + f:8 * g + f + 1]),
                                waits=(hstop[ui], const_tok, hgrp_reader))
                            st["p512"][bs + ui] = h_tok
                if CFG["dbgh"] and half == 1:
                    dcb = nc.dram_tensor("dbg_h_%d" % g,
                                         [128, 8192], BF16,
                                         kind="ExternalOutput").ap()
                    dump_tok, _ = ydma(dcb, hgrp[:, :], sem="out",
                                       waits=(h_tok,))
                for i4 in range(4):
                    odd2 = 10 + 2 * i4
                    for ui in range(2):
                        u = uoff + ui
                        tcm = None
                        for j in range(NCH):
                            tcm, _ = mm(p512[4][0:1, :],
                                        bass.AP(xnT, T * j + odd2,
                                                [[8 * T, 128], [1, 1]]),
                                        hgrp[:, 1024 * j + 512 * ui:
                                             1024 * j + 512 * ui + 512],
                                        j == 0, j == NCH - 1,
                                        sem="t" if j == NCH - 1 else None,
                                        waits=(h_tok, st["p512"][4]))
                        ct, _ = vop(lambda eng, i4=i4, ui=ui:
                                    eng.tensor_scalar_mul(
                                        crow4[0:1, 1024 * i4 + 512 * ui:
                                              1024 * i4 + 512 * ui + 512],
                                        p512[4][0:1, :],
                                        scgain[0:1, 0:1]),
                                    waits=(tcm,))
                        st["p512"][4] = ct
                c_tok[half] = ct
                for o in range(8):
                    bs = 0 if o % 2 == 0 else 2
                    OL = CFG["olora"]
                    for j in range(NCH):
                        lhs = ring.get(("w2", (g, o, j)))
                        ltok = ring.load_tok
                        lrec = None
                        for ui in range(2):
                            tok, lrec = mm(
                                p512[bs + ui][:, :], lhs,
                                hgrp[:, 1024 * j + 512 * ui:
                                     1024 * j + 512 * ui + 512],
                                j == 0, False if OL else j == NCH - 1,
                                sem=None if OL or j < NCH - 1 else "t",
                                waits=(h_tok, ltok, st["p512"][bs + ui]))
                        ring.used(lrec, tok)
                    hgrp_reader = [p.tok("t"), dump_tok]
                    i4 = o // 2
                    lsl = lrows4[0:1, 1024 * i4 + 256 * g + 128 * (o % 2):
                                 1024 * i4 + 256 * g + 128 * (o % 2) + 128]
                    for ui in range(2):
                        u = uoff + ui
                        if OL:
                            tl2, _ = mm(p512[bs + ui][:, :], lsl,
                                        crow4[0:1, 1024 * i4 + 512 * ui:
                                              1024 * i4 + 512 * ui + 512],
                                        False, True, sem="t",
                                        waits=(c_tok[half], l4_tok))
                        else:
                            tl2 = tok if ui == 1 else p.tok("t")
                        if g == 3:
                            ta, _ = vop(lambda eng, o=o, u=u, bk=bs + ui:
                                        eng.scalar_tensor_tensor(
                                            xt(o, 512 * u, 512),
                                            p512[bk][:, :],
                                            b2c[:, o:o + 1],
                                            xt(o, 512 * u, 512),
                                            ALU.add, ALU.add),
                                        waits=(tl2, const_tok))
                        else:
                            ta, _ = vop(lambda eng, o=o, u=u, bk=bs + ui:
                                        eng.tensor_add(
                                            xt(o, 512 * u, 512),
                                            xt(o, 512 * u, 512),
                                            p512[bk][:, :]),
                                        waits=(tl2,))
                        st["p512"][bs + ui] = ta
                        xadd = ta
        return xadd

    prev_store = None
    for sq in range(CFG["seqs"]):
        xl_tok, _ = ydma(xT[:, :], xv(sq), sem="xl",
                         waits=(prev_store, st["x_readers"]))
        st["x_ready"] = xl_tok
        for layer in range(CFG["layers"]):
            ln_tok = phase_ln(layer)
            toks = [ln_tok]
            if CFG["attn"]:
                ot_tok = phase_attention(layer, ln_tok)
                if CFG["outproj"]:
                    toks.append(phase_outproj(layer, ot_tok))
                else:
                    toks.append(ot_tok)
            if CFG["ffn"]:
                if layer == 0:
                    toks.append(phase_moe(ln_tok))
                else:
                    toks.append(phase_ffn1(ln_tok))
            st["x_ready"] = toks
        st_tok, _ = ydma(yv(sq), xT[:, :], sem="out", waits=(st["x_ready"],))
        prev_store = st_tok
        st["x_readers"] = st_tok
    ring.finish()
    n_out = CFG["seqs"] + (4 if CFG["dbgh"] else 0)
    if CFG["dbg"]:
        dbgs = [("xnT", xnT, BF16), ("qT", qT, BF16), ("kT", kT, BF16),
                ("vP", vP, BF16), ("OT", OT, BF16), ("mg", mg_sb, F32),
                ("er", ering, BF16), ("kcol", kcol, F32),
                ("mrow", mrow16, BF16), ("krow", krow16, BF16),
                ("mcol", mcolneg, F32), ("hb", hbuf, BF16),
                ("comb", comb, F32), ("sc2", scratch2, BF16),
                ("mu01", mu01, F32), ("lnrow", lnrow, F32)]
        allw = (p.tok("t"), p.tok("v"), p.tok("s"))
        for nm, h, dt_ in dbgs:
            shp = list(h.shape)
            d = nc.dram_tensor("dbg_" + nm, shp, dt_, kind="ExternalOutput").ap()
            ydma(d, h[:, :], sem="out", waits=allw)
            n_out += 1
    p.emit("y", lambda eng: eng.wait_ge(sems["out"], 16 * n_out))

    for e_ in "tvsgy":
        for _fn, _incs in p.q[e_]:
            assert len(_incs) <= 1, (e_, _incs)
    with nc.Block() as block:
        def replay(eng, lst):
            for fn, incs in lst:
                ins = fn(eng)
                for h, a in incs:
                    ins.then_inc(h, a)

        @block.tensor
        def _(eng):
            replay(eng, p.q["t"])

        @block.vector
        def _(eng):
            replay(eng, p.q["v"])

        @block.scalar
        def _(eng):
            replay(eng, p.q["s"])

        @block.gpsimd
        def _(eng):
            replay(eng, p.q["g"])

        @block.sync
        def _(eng):
            replay(eng, p.q["y"])

    es.close()
    return nc, ring.plan


def _host_tables():
    theta = np.exp(np.float32(np.log(np.float32(10000.0))) *
                   np.arange(HD // 2, dtype=np.float32) /
                   np.float32(HD)).astype(np.float32)
    args = (np.arange(T, dtype=np.float32)[:, None] * theta[None, :]).astype(np.float32)
    c = np.cos(args).astype(np.float32)
    s = np.sin(args).astype(np.float32)
    ropeC = np.zeros((128, T), dtype=np.float32)
    ropeS = np.zeros((128, T), dtype=np.float32)
    for r in range(128):
        d = r % HD
        j = d // 2
        ropeC[r] = c[:, j]
        ropeS[r] = (-s[:, j]) if (d % 2 == 0) else s[:, j]
    masks = np.zeros((4, 128, 512), dtype=BF)
    for dd in range(4):
        i = np.arange(128)[:, None]
        j = np.arange(512)[None, :]
        masks[dd] = (j >= i + 128 * dd).astype(BF)
    pswap = np.zeros((128, 128), dtype=BF)
    for i in range(128):
        pswap[i ^ 1, i] = 1.0
    return ropeC.astype(np.float16), ropeS.astype(np.float16), masks, pswap


def _chunk_weights(inp):
    w = {}

    def colchunk(a):
        return np.ascontiguousarray(
            a.reshape(a.shape[0] // 128, 128, a.shape[1] // 128, 128)
            .transpose(0, 2, 3, 1))

    w["wq"] = colchunk(np.asarray(inp["L0_attn_wq"]))
    w["wk"] = colchunk(np.asarray(inp["L0_attn_wk"]))
    w["wv"] = colchunk(np.asarray(inp["L0_attn_wv"]))
    w["wo"] = colchunk(np.asarray(inp["L0_attn_wo"]))
    wu_all = np.asarray(inp["L0_ffn_w_up"])
    w["wg"] = np.ascontiguousarray(
        wu_all[:, :HALF, :].reshape(E, 12, 128, 8, 128).transpose(0, 1, 3, 4, 2))
    w["wu"] = np.ascontiguousarray(
        wu_all[:, HALF:, :].reshape(E, 12, 128, 8, 128).transpose(0, 1, 3, 4, 2))
    w["wd"] = np.ascontiguousarray(
        np.asarray(inp["L0_ffn_w_down"]).reshape(E, 8, 128, 12, 128)
        .transpose(0, 1, 3, 4, 2))
    w["sg"] = np.ascontiguousarray(
        np.asarray(inp["L0_ffn_sg_gate"]).reshape(12, 128, 8, 128)
        .transpose(0, 2, 3, 1))
    w["su"] = np.ascontiguousarray(
        np.asarray(inp["L0_ffn_sg_up"]).reshape(12, 128, 8, 128)
        .transpose(0, 2, 3, 1))
    w["sd"] = np.ascontiguousarray(
        np.asarray(inp["L0_ffn_sg_down"]).reshape(8, 128, 12, 128)
        .transpose(0, 2, 3, 1))
    wat = np.asarray(inp["L1_attn_W_attn"])
    for i, nm in enumerate(("q1", "k1", "v1")):
        blk = wat[:, 1024 * i:1024 * (i + 1)]
        w[nm] = np.ascontiguousarray(
            blk.reshape(8, 128, 8, 128).transpose(2, 0, 1, 3))
    w["wp"] = np.ascontiguousarray(
        np.asarray(inp["L1_attn_W_proj"]).reshape(8, 128, 8, 128)
        .transpose(2, 0, 1, 3))
    w["w1"] = np.ascontiguousarray(
        np.asarray(inp["L1_ffn_W1"]).reshape(4, 8, 128, 8, 128)
        .transpose(0, 1, 3, 4, 2))
    w["w2"] = np.ascontiguousarray(
        np.asarray(inp["L1_ffn_W2"]).reshape(8, 128, 4, 8, 128)
        .transpose(2, 0, 3, 4, 1))
    return w


_CACHE = {}


def kernel(**inputs):
    if "prog" not in _CACHE:
        _CACHE["prog"] = build()
    nc, plan = _CACHE["prog"]
    if CFG["seqs"] == SPC and CFG["layers"] == 2 and CFG["attn"] \
            and CFG["outproj"] and CFG["ffn"]:
        assert len(plan) == 2 * NCHUNK, len(plan)
        assert plan[:NCHUNK] == plan[NCHUNK:]
    w = _chunk_weights(inputs)
    nuse = min(len(plan), NCHUNK)
    stream = np.zeros((NCHUNK, 128, 128), dtype=np.float32)
    for i, (nm, idx) in enumerate(plan[:nuse]):
        stream[i] = w[nm][idx]
    wg = np.ascontiguousarray(
        stream.reshape(NG, GROUP, 128, 128).transpose(0, 2, 1, 3)
        .reshape(NG, 128, GROUP * 128))

    ropeC, ropeS, masks, pswap = _host_tables()
    ba = np.asarray(inputs["L1_attn_b_attn"], dtype=np.float32)
    bqk = np.ascontiguousarray(np.concatenate(
        [ba[0:1024].reshape(8, 128).T, ba[1024:2048].reshape(8, 128).T], axis=1))
    common = {
        "wstr": wg,
        "d_ropeC": ropeC, "d_ropeS": ropeS, "d_masks": masks,
        "d_pswap": pswap,
        "d_identb": np.eye(128, dtype=BF),
        "d_identf": np.eye(128, dtype=np.float32),
        "d_onesrowf": np.ones((1, 128), dtype=np.float32),
        "d_onesrow16": np.ones((1, 128), dtype=BF),
        "d_onescol16": np.ones((128, 1), dtype=BF),
        "d_onescolf": np.ones((128, 1), dtype=np.float32),
        "d_ones2d": np.ones((128, 64), dtype=np.float32),
        "d_gw": np.ascontiguousarray(
            np.asarray(inputs["L0_ffn_gate_w"], dtype=np.float32).T
            .reshape(8, 128, 8)),
        "d_bqk": bqk,
        "d_bv1r": np.ascontiguousarray(ba[2048:3072].reshape(1, 1024)).astype(BF),
        "d_bp1": np.ascontiguousarray(
            np.asarray(inputs["L1_attn_b_proj"], dtype=np.float32)
            .reshape(8, 128).T),
        "d_b1c": np.ascontiguousarray(
            np.asarray(inputs["L1_ffn_b1"], dtype=np.float32).reshape(32, 128).T),
        "d_b2c": np.ascontiguousarray(
            np.asarray(inputs["L1_ffn_b2"], dtype=np.float32).reshape(8, 128).T),
        "d_sc": np.array([[np.float32(inputs["L1_ffn_scale"]),
                           np.float32(inputs["L1_attn_score_gain"])]],
                         dtype=np.float32),
    }
    x = np.asarray(inputs["L0_attn_x"], dtype=np.float32)
    in_maps = []
    for c in range(NCORES):
        m = dict(common)
        m["x_in"] = np.ascontiguousarray(
            x[SPC * c:SPC * (c + 1)].transpose(0, 2, 1))
        in_maps.append(m)
    res = run_bass_kernel_spmd(nc, in_maps, core_ids=list(range(NCORES)))
    out = np.empty((B, T, D), dtype=np.float32)
    for c in range(NCORES):
        y = res.results[c]["y_out"]
        out[SPC * c:SPC * (c + 1)] = np.asarray(y).transpose(0, 2, 1)
    return out


# version 14
# speedup vs baseline: 1.0335x; 1.0335x over previous
import numpy as np
import ml_dtypes
import concourse.bass as bass
import concourse.mybir as mybir
from concourse.bass_utils import run_bass_kernel_spmd

F32 = mybir.dt.float32
BF16 = mybir.dt.bfloat16
FP16 = mybir.dt.float16
AF = mybir.ActivationFunctionType
ALU = mybir.AluOpType
AX = mybir.AxisListType

B, T, D = 16, 2048, 1024
NCORES = 8
SPC = B // NCORES
HD, NH, NPAIR = 64, 16, 8
HALF, E = 1536, 8
NSUP = T // 512
NCH = D // 128
NTT = T // 128

GROUP = 16
RDEPTH = 2
NCHUNK = 6720
NG = NCHUNK // GROUP

BF = ml_dtypes.bfloat16

CFG = {"seqs": SPC, "layers": 2, "attn": True, "outproj": True, "ffn": True,
       "pairs": NPAIR, "dbg": False, "olora": True, "dbgh": False}


class Plan:
    OWN = {"t": "t", "v": "v", "s": "s"}

    def __init__(self, sems):
        self.sems = sems
        self.q = {k: [] for k in "tvsgy"}
        self.cnt = {k: 0 for k in sems}
        self.lastw = {e: {} for e in "tvsgy"}

    def emit(self, e, fn):
        rec = [fn, []]
        self.q[e].append(rec)
        return rec

    def inc(self, rec, sem, amt=1):
        self.cnt[sem] += amt
        rec[1].append((self.sems[sem], amt))
        return (sem, self.cnt[sem])

    def tok(self, sem):
        return (sem, self.cnt[sem])

    def selfwait(self, e):
        s = Plan.OWN[e]
        v = self.cnt[s]
        if v <= 0:
            return
        h = self.sems[s]
        self.q[e].append([lambda eng, h=h, v=v: eng.wait_ge(h, v), []])

    def wait(self, e, tok):
        if tok is None:
            return
        if isinstance(tok, (list, tuple)) and tok and not isinstance(tok[0], str):
            for t in tok:
                self.wait(e, t)
            return
        s, v = tok
        if v <= 0 or Plan.OWN.get(e) == s:
            return
        if self.lastw[e].get(s, 0) >= v:
            return
        self.lastw[e][s] = v
        h = self.sems[s]
        self.q[e].append([lambda eng, h=h, v=v: eng.wait_ge(h, v), []])


class WRing:
    def __init__(self, p, sbuf, wdram):
        self.p = p
        self.sb = sbuf
        self.wdram = wdram
        self.plan = []
        self.gi = -1
        self.off = GROUP
        self.load_tok = None
        self.last_use = None
        self.free_toks = []

    def _close(self):
        rec, tok = self.last_use
        if tok is None:
            tok = self.p.inc(rec, "wf")
        self.free_toks.append(tok)
        self.last_use = None

    def get(self, key):
        p = self.p
        if self.off == GROUP:
            if self.last_use is not None:
                self._close()
            self.gi += 1
            self.off = 0
            g = self.gi
            if g >= RDEPTH:
                p.wait("g", self.free_toks[g - RDEPTH])
            slot = g % RDEPTH
            dst = self.sb[:, slot * GROUP * 128:(slot + 1) * GROUP * 128]
            src = self.wdram[g % NG]
            rec = p.emit("g", lambda eng, d=dst, s=src: eng.dma_start(d, s))
            self.load_tok = p.inc(rec, "w%d" % slot, 16)
        slot = self.gi % RDEPTH
        c = self.off
        self.off += 1
        self.plan.append(key)
        base = (slot * GROUP + c) * 128
        return self.sb[:, base:base + 128]

    def used(self, rec, tok=None):
        self.last_use = (rec, tok)

    def finish(self):
        if self.last_use is not None:
            self._close()


def build():
    nc = bass.Bass("TRN2", target_bir_lowering=False)
    x_in_h = nc.dram_tensor("x_in", [SPC, D, T], F32, kind="ExternalInput")
    y_out_h = nc.dram_tensor("y_out", [SPC, D, T], F32, kind="ExternalOutput")
    wstr = nc.dram_tensor("wstr", [NG, 128, GROUP * 128], F32,
                          kind="ExternalInput").ap()
    d_ropeC = nc.dram_tensor("d_ropeC", [128, T], FP16, kind="ExternalInput").ap()
    d_ropeS = nc.dram_tensor("d_ropeS", [128, T], FP16, kind="ExternalInput").ap()
    d_masks_h = nc.dram_tensor("d_masks", [4, 128, 512], BF16, kind="ExternalInput")
    d_pswap = nc.dram_tensor("d_pswap", [128, 128], BF16, kind="ExternalInput").ap()
    d_identb = nc.dram_tensor("d_identb", [128, 128], BF16, kind="ExternalInput").ap()
    d_identf = nc.dram_tensor("d_identf", [128, 128], F32, kind="ExternalInput").ap()
    d_onesrowf = nc.dram_tensor("d_onesrowf", [1, 128], F32, kind="ExternalInput").ap()
    d_onesrow16 = nc.dram_tensor("d_onesrow16", [1, 128], BF16, kind="ExternalInput").ap()
    d_onescol16 = nc.dram_tensor("d_onescol16", [128, 1], BF16, kind="ExternalInput").ap()
    d_onescolf = nc.dram_tensor("d_onescolf", [128, 1], F32, kind="ExternalInput").ap()
    d_ones2d = nc.dram_tensor("d_ones2d", [128, 64], F32, kind="ExternalInput").ap()
    d_gw_h = nc.dram_tensor("d_gw", [8, 128, 8], F32, kind="ExternalInput")
    d_bqk = nc.dram_tensor("d_bqk", [128, 16], F32, kind="ExternalInput").ap()
    d_bv1r = nc.dram_tensor("d_bv1r", [1, 1024], BF16, kind="ExternalInput").ap()
    d_bp1 = nc.dram_tensor("d_bp1", [128, 8], F32, kind="ExternalInput").ap()
    d_b1c = nc.dram_tensor("d_b1c", [128, 32], F32, kind="ExternalInput").ap()
    d_b2c = nc.dram_tensor("d_b2c", [128, 8], F32, kind="ExternalInput").ap()
    d_sc = nc.dram_tensor("d_sc", [1, 2], F32, kind="ExternalInput").ap()

    LAYOUT_BYTES = 206 * 1024 + 512
    backing = nc.alloc_sbuf_tensor("layout_backing", [128, LAYOUT_BYTES],
                                   mybir.dt.uint8)
    BASE = nc.lookup_mloc(backing).addr
    off = [BASE]

    def sz(shape, dtype):
        n = 1
        for d in shape[1:]:
            n *= d
        return n * (2 if dtype in (BF16, FP16) else 4)

    def sbal(name, shape, dtype, pos):
        o = (pos[0] + 63) // 64 * 64
        h = nc.alloc_sbuf_tensor_at(name, list(shape), dtype, offset=o)
        pos[0] = o + sz(shape, dtype)
        return h

    def sb(name, shape, dtype):
        return sbal(name, shape, dtype, off)

    xT = sb("xT", [128, 8 * T], F32)
    xnT = sb("xnT", [128, 8 * T], BF16)
    OT = sb("OT", [128, 8 * T], BF16)
    wring = sb("wring", [128, RDEPTH * GROUP * 128], BF16)
    identb = sb("identb", [128, 128], BF16)
    identf = sb("identf", [128, 128], F32)
    pswap = sb("pswap", [128, 128], BF16)
    onesrowf = sb("onesrowf", [1, 128], F32)
    onesrow16 = sb("onesrow16", [1, 128], BF16)
    onescol16 = sb("onescol16", [128, 1], BF16)
    onescolf = sb("onescolf", [128, 1], F32)
    ones2d = sb("ones2d", [128, 64], F32)
    gw16 = sb("gw16", [128, 64], BF16)
    bqk = sb("bqk", [128, 16], F32)
    bp1 = sb("bp1", [128, 8], F32)
    b1c = sb("b1c", [128, 32], F32)
    b2c = sb("b2c", [128, 8], F32)
    scgain = sb("scgain", [1, 2], F32)
    gainB = sb("gainB", [128, 1], F32)
    mg_sb = sb("mg_sb", [128, 8], F32)
    kbar = sb("kbar", [128, 2], F32)
    kbar16 = sb("kbar16", [128, 2], BF16)
    mcolneg = sb("mcolneg", [128, 32], F32)
    madacc = sb("madacc", [128, 128], F32)
    madsum = sb("madsum", [128, 32], F32)
    kcol = sb("kcol", [128, 32], F32)
    mu01 = sb("mu01", [1, 8], F32)
    mu16 = sb("mu16", [1, 1], BF16)
    comb = sb("comb", [128, 128], F32)
    x2tmp = sb("x2tmp", [128, 1024], F32)
    lntmp = sb("lntmp", [128, 512], F32)
    lnrow = sb("lnrow", [1, 2 * 512], F32)

    region0 = (off[0] + 63) // 64 * 64
    REGION = 56 * 1024
    LIMIT = region0 + REGION
    assert LIMIT <= BASE + LAYOUT_BYTES, (BASE, LIMIT)

    roff = [region0]

    def rb(name, shape, dtype):
        h = sbal(name, shape, dtype, roff)
        assert roff[0] <= LIMIT, (name, roff[0])
        return h

    ropeC = rb("ropeC", [128, T], FP16)
    ropeS = rb("ropeS", [128, T], FP16)
    qT = rb("qT", [128, T], BF16)
    kT = rb("kT", [128, T], BF16)
    vP = rb("vP", [128, NTT * 130], BF16)
    scratch2 = rb("scratch2", [128, T], BF16)
    vtmp = scratch2
    ering = rb("ering", [128, 4 * 512], BF16)
    masksb = rb("masksb", [128, 2048], BF16)
    zring = rb("zring", [128, 2 * 512], F32)
    u2ring = rb("u2ring", [128, 2 * 512], F32)
    kb_sb = rb("kb_sb", [128, 2 * 512], F32)
    rbuf = rb("rbuf", [128, 512], F32)
    recB = rb("recB", [64, 512], F32)
    mrow16 = rb("mrow16", [1, 2 * 512], BF16)
    krow16 = rb("krow16", [1, 2 * 512], BF16)
    bvB = rb("bvB", [128, 128], BF16)
    bv1r = rb("bv1r", [1, 1024], BF16)

    roff[0] = region0
    hbuf = rb("hbuf", [128, 12 * 1024], BF16)
    combB = rb("combB", [128, T], BF16)
    sgt = rb("sgt", [128, 4 * 512], BF16)
    crow16 = rb("crow16", [1, T], BF16)

    roff[0] = region0
    hgrp = rb("hgrp", [128, 8 * 1024], BF16)
    lrows = rb("lrows", [1, 8 * 1024], BF16)
    arow = rb("arow", [1, T], BF16)
    crow4 = rb("crow4", [1, 4 * 1024], BF16)
    lrows4 = rb("lrows4", [1, 4096], BF16)

    import contextlib
    es = contextlib.ExitStack()
    p512 = [es.enter_context(nc.psum_tensor("p512_%d" % i, [128, 512], F32))
            for i in range(5)]
    pA = es.enter_context(nc.psum_tensor("pA", [128, 1024], F32))
    pbf = es.enter_context(nc.psum_tensor("pbf", [128, 512], BF16))
    sems = {s_: es.enter_context(nc.semaphore("sem_" + s_))
            for s_ in ("w0", "w1", "gw", "io", "xl", "tbl", "out",
                       "t", "v", "s", "wf")}

    p = Plan(sems)
    ring = WRing(p, wring, wstr)

    def emt(e, fn, sem=None, waits=(), amt=1):
        for w in waits:
            p.wait(e, w)
        rec = p.emit(e, fn)
        if sem:
            return p.inc(rec, sem, amt), rec
        return None, rec

    def mm(out_ap, lhs_ap, rhs_ap, start, stop, tp=None, sem=None, waits=()):
        return emt("t", lambda eng, o=out_ap, l=lhs_ap, r=rhs_ap, a=start, b=stop,
                 t_=tp: eng.matmul(o, l, r, start=a, stop=b, tile_position=t_,
                                   skip_group_check=True),
                 sem=sem, waits=waits)

    def wmm(out_ap, key, rhs_ap, start, stop, tp=None, sem=None, waits=()):
        lhs = ring.get(key)
        tok, rec = mm(out_ap, lhs, rhs_ap, start, stop, tp, sem,
                      tuple(waits) + (ring.load_tok,))
        ring.used(rec, tok)
        return tok, rec

    def vop(fn, sem="v", waits=(), sw=False):
        p.selfwait("v")
        return emt("v", fn, sem=sem, waits=waits)

    def sop(fn, sem="s", waits=(), sw=False):
        p.selfwait("s")
        return emt("s", fn, sem=sem, waits=waits)

    def xn(c, t0, n):
        return xnT[:, T * c + t0: T * c + t0 + n]

    def xt(c, t0, n):
        return xT[:, T * c + t0: T * c + t0 + n]

    def ydma(dst, src, sem="io", waits=()):
        return emt("y", lambda eng, d=dst, s=src: eng.dma_start(d, s),
                   sem=sem, waits=waits, amt=16)

    for dst, src in ((identb[:, :], d_identb), (identf[:, :], d_identf),
                     (pswap[:, :], d_pswap), (onesrowf[:, :], d_onesrowf),
                     (onesrow16[:, :], d_onesrow16), (onescol16[:, :], d_onescol16),
                     (onescolf[:, :], d_onescolf), (ones2d[:, :], d_ones2d),
                     (bqk[:, :], d_bqk), (bp1[:, :], d_bp1),
                     (b1c[:, :], d_b1c), (b2c[:, :], d_b2c),
                     (scgain[:, :], d_sc)):
        ydma(dst, src)
    _, grec = emt("g", lambda eng: eng.dma_start(gw16[:, :], gw_src))
    gw_tok = p.inc(grec, "gw", 16)
    const_tok = p.tok("io")

    def xv(sq):
        return bass.AP(x_in_h, sq * D * T, [[T, 128], [128 * T, NCH], [1, T]])

    def yv(sq):
        return bass.AP(y_out_h, sq * D * T, [[T, 128], [128 * T, NCH], [1, T]])

    masks_src = bass.AP(d_masks_h, 0, [[512, 128], [128 * 512, 4], [1, 512]])
    gw_src = bass.AP(d_gw_h, 0, [[8, 128], [128 * 8, 8], [1, 8]])

    gB_t, _ = mm(p512[0][:, 0:1], onesrowf[:, :], scgain[:, 1:2], True, True,
                 sem="t", waits=(const_tok,))
    gB_v, _ = vop(lambda eng: eng.tensor_copy(gainB[:, :], p512[0][:, 0:1]),
                  waits=(gB_t,))

    st = {"x_ready": None, "x_readers": None, "x2": [None, None],
          "p512": [gB_v, None, None, None, None], "pA": None, "pbf": None}

    def phase_ln(layer):
        pm_tok = st["p512"][4]
        pq_tok = st["pA"]
        last = None
        for u in range(NSUP):
            pm = p512[4][0:1, :]
            pq = pA[0:1, 0:512]
            for c in range(NCH):
                tok, _ = mm(pm, onescolf[:, :], xt(c, 512 * u, 512),
                            c == 0, c == NCH - 1,
                            sem="t" if c == NCH - 1 else None,
                            waits=(st["x_ready"], pm_tok))
            sum_tok = tok
            for c in range(NCH):
                sl = (c % 2) * 512
                stok, _ = sop(lambda eng, c=c, u=u, sl=sl: eng.activation(
                    x2tmp[:, sl:sl + 512], xt(c, 512 * u, 512), AF.Square),
                    waits=(st["x_ready"], st["x2"][c % 2]))
                ttok, _ = mm(pq, onescolf[:, :], x2tmp[:, sl:sl + 512],
                             c == 0, c == NCH - 1,
                             sem="t", waits=(stok, pq_tok))
                st["x2"][c % 2] = ttok
            ssq_tok = ttok
            mrow = lnrow[0:1, 0:512]
            vrow = lnrow[0:1, 512:1024]
            rrow = vrow
            t1, _ = vop(lambda eng: eng.tensor_scalar_mul(mrow, pm, 1.0 / D),
                        waits=(sum_tok,))
            t2, _ = vop(lambda eng: eng.tensor_mul(vrow, mrow, mrow), sw=True)
            t3, _ = vop(lambda eng: eng.scalar_tensor_tensor(
                vrow, pq, 1.0 / D, vrow, ALU.mult, ALU.subtract),
                waits=(ssq_tok,))
            t3b, _ = vop(lambda eng: eng.tensor_scalar_add(vrow, vrow, 1e-5),
                         sw=True)
            t4, _ = sop(lambda eng: eng.activation(rrow, vrow, AF.Sqrt),
                        waits=(t3b,))
            t5, _ = vop(lambda eng: eng.reciprocal(rrow, rrow), waits=(t4,),
                        sw=True)
            tb1, _ = mm(p512[4][:, :], onesrowf[:, :], mrow, True, True,
                        sem="t", waits=(t1, t5))
            tb2, _ = mm(pA[:, 0:512], onesrowf[:, :], rrow, True, True,
                        sem="t", waits=(t5,))
            for c in range(NCH):
                tv1, _ = vop(lambda eng, c=c, u=u: eng.tensor_sub(
                    lntmp[:, :], xt(c, 512 * u, 512), p512[4][:, :]),
                    waits=(tb1,))
                tv2, _ = vop(lambda eng, c=c, u=u: eng.tensor_mul(
                    xn(c, 512 * u, 512), lntmp[:, :], pA[:, 0:512]),
                    waits=(tb2,))
            pm_tok = tv1
            pq_tok = tv2
            last = tv2
        st["p512"][4] = pm_tok
        st["pA"] = pq_tok
        return last

    def phase_attention(layer, xn_tok):
        L0 = layer == 0
        ot_last = None
        sc2_tok = None
        allw = (p.tok("t"), p.tok("v"), p.tok("s"))
        ydma(masksb[:, :], masks_src, sem="tbl", waits=allw)
        if L0:
            ydma(ropeC[:, :], d_ropeC, sem="tbl")
            ydma(ropeS[:, :], d_ropeS, sem="tbl")
        else:
            ydma(bv1r[:, :], d_bv1r, sem="tbl")
        mask_tok = rope_tok = bv_tok = p.tok("tbl")
        for pair in range(CFG["pairs"]):
            names = ("wq", "wk", "wv") if L0 else ("q1", "k1", "v1")
            done = {}
            for wi, wnm in enumerate(names):
                dstbuf = (qT, kT, vtmp)[wi]
                utoks = [None] * NSUP
                for k in range(NCH):
                    lhs = ring.get((wnm, (pair, k)))
                    ltok = ring.load_tok
                    lrec = None
                    for u in range(NSUP):
                        tok, lrec = mm(p512[u][:, :], lhs,
                                       xn(k, 512 * u, 512), k == 0, k == NCH - 1,
                                       sem="t" if k == NCH - 1 else None,
                                       waits=(xn_tok, ltok, st["p512"][u]))
                        if k == NCH - 1:
                            utoks[u] = tok
                    ring.used(lrec, tok if tok else None)
                etok = None
                for u in range(NSUP):
                    if (not L0) and wi < 2:
                        bcol = bqk[:, 8 * wi + pair:8 * wi + pair + 1]
                        etok, _ = vop(lambda eng, d=dstbuf, u=u, bc=bcol:
                                      eng.tensor_scalar_add(
                                          d[:, 512 * u:512 * u + 512],
                                          p512[u][:, :], bc),
                                      waits=(utoks[u], const_tok))
                    else:
                        etok, _ = vop(lambda eng, d=dstbuf, u=u:
                                      eng.tensor_copy(
                                          d[:, 512 * u:512 * u + 512],
                                          p512[u][:, :]),
                                      waits=(utoks[u],))
                    st["p512"][u] = etok
                done[wnm] = etok
            q_done = done[names[0]]
            k_done = done[names[1]]
            v_done = done[names[2]]

            if L0:
                newdone = []
                for buf, dtok in ((qT, q_done), (kT, k_done)):
                    t4 = None
                    for u in range(NSUP):
                        t1, _ = mm(p512[4][:, :], pswap[:, :],
                                   buf[:, 512 * u:512 * u + 512], True, True,
                                   sem="t", waits=(dtok, const_tok, rope_tok,
                                                   st["p512"][4]))
                        t2, _ = vop(lambda eng, u=u: eng.tensor_mul(
                            zring[:, 0:512], p512[4][:, :],
                            ropeS[:, 512 * u:512 * u + 512]),
                            waits=(t1, rope_tok))
                        st["p512"][4] = t2
                        t3, _ = vop(lambda eng, buf=buf, u=u: eng.tensor_mul(
                            zring[:, 512:1024], buf[:, 512 * u:512 * u + 512],
                            ropeC[:, 512 * u:512 * u + 512]))
                        t4, _ = vop(lambda eng, buf=buf, u=u: eng.tensor_add(
                            buf[:, 512 * u:512 * u + 512], zring[:, 0:512],
                            zring[:, 512:1024]))
                    newdone.append(t4)
                q_done, k_done = newdone

            mz, _ = vop(lambda eng: eng.memset(
                bass.AP(vP, 64, [[130 * NTT, 128], [65, 2 * NTT]]), 1.0))
            tbv2 = None
            if not L0:
                tbv, _ = mm(p512[4][:, 0:128], onesrow16[:, :],
                            bv1r[0:1, 128 * pair:128 * pair + 128], True, True,
                            sem="t", waits=(const_tok, bv_tok, st["p512"][4]))
                tbv2, _ = vop(lambda eng: eng.tensor_copy(
                    bvB[:, :], p512[4][:, 0:128]), waits=(tbv,))
                st["p512"][4] = tbv2
            vp_last = None
            for tt in range(NTT):
                sl = (tt % 4) * 128
                ttr, _ = emt("t", lambda eng, tt=tt, sl=sl: eng.transpose(
                    pbf[:, sl:sl + 128], vtmp[:, 128 * tt:128 * tt + 128],
                    identb[:, :]), sem="t",
                    waits=(v_done, const_tok, st["pbf"]))
                dst = bass.AP(vP, 130 * tt, [[130 * NTT, 128], [65, 2], [1, 64]])
                srcap = bass.AP(pbf, sl, [[512, 128], [64, 2], [1, 64]])
                if L0:
                    tv, _ = vop(lambda eng, d=dst, s=srcap: eng.tensor_copy(d, s),
                                waits=(ttr, mz))
                else:
                    bap = bass.AP(bvB, 0, [[128, 128], [64, 2], [1, 64]])
                    tv, _ = vop(lambda eng, d=dst, s=srcap, b=bap:
                                eng.tensor_tensor(d, s, b, ALU.add),
                                waits=(ttr, mz, tbv2))
                st["pbf"] = tv
                vp_last = tv
            vp_done = vp_last

            if L0:
                for qi, (buf, dtok) in enumerate(((qT, q_done), (kT, k_done))):
                    sqk, _ = sop(lambda eng, buf=buf: eng.activation(
                        scratch2[:, :], buf[:, :], AF.Square),
                        waits=(dtok, sc2_tok, vp_done))
                    for u in range(NSUP):
                        tn, _ = mm(pA[0:1, 512:1024], onescol16[:, :],
                                   scratch2[:, 512 * u:512 * u + 512], True, True,
                                   sem="t", waits=(sqk, st["pA"]))
                        tm, _ = vop(lambda eng, u=u: eng.reduce_max(
                            mu01[0:1, u:u + 1], pA[0:1, 512:1024], AX.X),
                            waits=(tn,))
                        st["pA"] = tm
                    tmx, _ = vop(lambda eng, qi=qi: eng.reduce_max(
                        mu01[0:1, 4 + qi:5 + qi], mu01[0:1, 0:4], AX.X),
                        sw=True)
                    sc2_tok = p.tok("t")
                tadd, _ = vop(lambda eng: eng.tensor_mul(
                    mu01[0:1, 6:7], mu01[0:1, 4:5], mu01[0:1, 5:6]))
                tsq, _ = sop(lambda eng: eng.activation(
                    mu01[0:1, 7:8], mu01[0:1, 6:7], AF.Sqrt), waits=(tadd,))
                tsc, _ = vop(lambda eng: eng.tensor_scalar_mul(
                    mu01[0:1, 6:7], mu01[0:1, 7:8], -1.0 / 8.0), waits=(tsq,))
                t16, _ = vop(lambda eng: eng.tensor_copy(mu16[:, :],
                                                         mu01[0:1, 6:7]))
                tmg, _ = mm(p512[4][:, 0:1], onesrow16[:, :], mu16[:, :],
                            True, True, sem="t", waits=(t16, st["p512"][4]))
                tmg2, _ = vop(lambda eng, pair=pair: eng.tensor_copy(
                    mg_sb[:, pair:pair + 1], p512[4][:, 0:1]), waits=(tmg,))
                st["p512"][4] = tmg2
                stats_tok = tmg2
            else:
                tkb, _ = vop(lambda eng: eng.reduce_sum(
                    kbar[:, 0:1], kT[:, :], AX.X), waits=(k_done,))
                tkb2, _ = vop(lambda eng: eng.tensor_copy(
                    kbar16[:, 0:1], kbar[:, 0:1]), sw=True)
                for hh in range(2):
                    for tt in range(NTT):
                        tmc, _ = mm(p512[4][:, tt:tt + 1],
                                    qT[64 * hh:64 * hh + 64,
                                       128 * tt:128 * tt + 128],
                                    kbar16[64 * hh:64 * hh + 64, 0:1],
                                    True, True,
                                    sem="t" if tt == NTT - 1 else None,
                                    waits=(tkb2, q_done, st["p512"][4]))
                    tmc2, _ = vop(lambda eng, hh=hh: eng.tensor_scalar_mul(
                        mcolneg[:, 16 * hh:16 * hh + 16], p512[4][:, 0:NTT],
                        -1.0 / T), waits=(tmc,))
                    st["p512"][4] = tmc2
                mstat_tok = tmc2
                half_ring = [pA[:, 0:512], pA[:, 512:1024]] + \
                    [p512[i][:, :] for i in range(4)]
                hr_tok = [st["pA"], st["pA"]] + [st["p512"][i] for i in range(4)]
                hi_ = 0
                for tt in range(NTT):
                    for hh in range(2):
                        for half in range(4):
                            slot = hi_ % 6
                            ps = half_ring[slot]
                            tsm, _ = mm(ps, qT[64 * hh:64 * hh + 64,
                                               128 * tt:128 * tt + 128],
                                        kT[64 * hh:64 * hh + 64,
                                           512 * half:512 * half + 512],
                                        True, True, tp=(64 * hh, 0), sem="t",
                                        waits=(hr_tok[slot], q_done, k_done,
                                               mstat_tok))
                            acol = madacc[:, 64 * hh + 4 * tt + half:
                                          64 * hh + 4 * tt + half + 1]
                            bcol = mcolneg[:, 16 * hh + tt:16 * hh + tt + 1]
                            tab, _ = sop(lambda eng, ps=ps, a=acol, bc=bcol,
                                         half=half, hh=hh: eng.activation(
                                             scratch2[:, 1024 * hh +
                                                      512 * (half % 2):
                                                      1024 * hh +
                                                      512 * (half % 2) + 512],
                                             ps, AF.Abs, bias=bc,
                                             accum_out=a),
                                         waits=(tsm,))
                            hr_tok[slot] = tab
                            hi_ += 1
                st["pA"] = [hr_tok[0], hr_tok[1]]
                for i in range(4):
                    st["p512"][i] = hr_tok[2 + i]
                mad_tail = [hr_tok[i] for i in range(6)]
                tms = []
                for hh in range(2):
                    aps = [bass.AP(madacc, 64 * hh + q_, [[128, 128], [4, 16]])
                           for q_ in range(4)]
                    ms = madsum[:, 16 * hh:16 * hh + 16]
                    tq1, _ = vop(lambda eng, a=aps, ms=ms: eng.tensor_add(
                        ms, a[0], a[1]), waits=tuple(mad_tail))
                    tq2, _ = vop(lambda eng, a=aps: eng.tensor_add(
                        x2tmp[:, 0:16], a[2], a[3]))
                    tq3, _ = vop(lambda eng, ms=ms: eng.tensor_add(
                        ms, ms, x2tmp[:, 0:16]), sw=True)
                    tms.append(tq3)
                tk1, _ = vop(lambda eng: eng.tensor_scalar(
                    kcol[:, :], madsum[:, :], 1.0 / T, 1e-6, ALU.mult, ALU.add),
                    waits=tuple(tms), sw=True)
                tk2, _ = vop(lambda eng: eng.reciprocal(kcol[:, :], kcol[:, :]),
                             sw=True)
                tk3, _ = vop(lambda eng: eng.tensor_scalar_mul(
                    kcol[:, :], kcol[:, :], gainB[:, 0:1]), waits=(gB_v,),
                    sw=True)
                stats_tok = tk3

            ztok = [None, None]
            u2tok = [None, None]
            etok_slot = [None, None, None, None]
            for u in range(NSUP):
                ntc = 4 * (u + 1)
                kb_toks = [None, None]
                mrow_toks = [None, None]
                if not L0:
                    for hh in range(2):
                        tmr, _ = mm(p512[4][0:1, :],
                                    kbar16[64 * hh:64 * hh + 64, 0:1],
                                    qT[64 * hh:64 * hh + 64,
                                       512 * u:512 * u + 512],
                                    True, True, sem="t",
                                    waits=(tkb2, q_done, st["p512"][4]))
                        tmr2, _ = vop(lambda eng, hh=hh:
                                      eng.tensor_scalar_mul(
                                          mrow16[0:1, 512 * hh:512 * hh + 512],
                                          p512[4][0:1, :], -1.0 / T),
                                      waits=(tmr,))
                        st["p512"][4] = tmr2
                        mrow_toks[hh] = tmr2
                        ttr2 = None
                        for j in range(4):
                            tt_ = 4 * u + j
                            ttr2, _ = emt("t", lambda eng, hh=hh, tt_=tt_, j=j:
                                        eng.transpose(
                                            pA[0:1, 512 + 128 * j:
                                               512 + 128 * j + 128],
                                            kcol[:, 16 * hh + tt_:
                                                 16 * hh + tt_ + 1],
                                            identf[:, :]),
                                        sem="t", waits=(stats_tok, st["pA"]))
                        tkr, _ = vop(lambda eng, hh=hh: eng.tensor_copy(
                            krow16[0:1, 512 * hh:512 * hh + 512],
                            pA[0:1, 512:1024]), waits=(ttr2,))
                        st["pA"] = tkr
                        pk_ap = pA[:, 512:1024] if hh == 1 else p512[4][:, :]
                        pk_tok = st["pA"] if hh == 1 else st["p512"][4]
                        tkb_, _ = mm(pk_ap, onesrow16[:, :],
                                     krow16[0:1, 512 * hh:512 * hh + 512],
                                     True, True, sem="t",
                                     waits=(tkr, pk_tok))
                        tkb2_, _ = vop(lambda eng, hh=hh, a=pk_ap:
                                       eng.tensor_copy(
                                           kb_sb[:, 512 * hh:512 * hh + 512], a),
                                       waits=(tkb_,))
                        if hh == 1:
                            st["pA"] = tkb2_
                        else:
                            st["p512"][4] = tkb2_
                        kb_toks[hh] = tkb2_
                pv_ps = [p512[4][0:65, :], pA[0:65, 0:512]]
                pv_ps_wait = [st["p512"][4], st["pA"]]
                pv_started = [False, False]
                pv_tok = [None, None]
                pend = []

                def do_pv(hh, tc, e_ap, e_tok):
                    tokpv, _ = mm(pv_ps[hh],
                                  vP[:, 130 * tc + 65 * hh:
                                     130 * tc + 65 * hh + 65],
                                  e_ap, not pv_started[hh], tc == ntc - 1,
                                  sem="t",
                                  waits=(e_tok, vp_done,
                                         pv_ps_wait[hh] if not pv_started[hh]
                                         else None))
                    pv_started[hh] = True
                    pv_tok[hh] = tokpv
                    return tokpv

                for tc in range(ntc):
                    for hh in range(2):
                        sring = tc % 2 + 2 * hh
                        ps = p512[sring]
                        tsm, _ = mm(ps[:, :],
                                    kT[64 * hh:64 * hh + 64,
                                       128 * tc:128 * tc + 128],
                                    qT[64 * hh:64 * hh + 64,
                                       512 * u:512 * u + 512],
                                    True, L0, tp=(64 * hh, 0),
                                    sem="t" if L0 else None,
                                    waits=(q_done, k_done, stats_tok,
                                           st["p512"][sring]))
                        if not L0:
                            tsm, _ = mm(ps[:, :], onesrow16[:, :],
                                        mrow16[0:1, 512 * hh:512 * hh + 512],
                                        False, True, sem="t",
                                        waits=(mrow_toks[hh],))
                        esl = (2 * tc + hh) % 4
                        e_ap = ering[:, 512 * esl:512 * esl + 512]
                        dmask = tc - 4 * u
                        if L0:
                            ertok, _ = sop(lambda eng, ps=ps, e=e_ap, pair=pair:
                                           eng.activation(
                                               e, ps[:, :], AF.Exp,
                                               bias=mg_sb[:, pair:pair + 1],
                                               scale=0.125),
                                           waits=(tsm, stats_tok,
                                                  etok_slot[esl]))
                        else:
                            zq = hh
                            tz, _ = vop(lambda eng, ps=ps, zq=zq, hh=hh:
                                        eng.tensor_mul(
                                            zring[:, 512 * zq:512 * zq + 512],
                                            ps[:, :],
                                            kb_sb[:, 512 * hh:512 * hh + 512]),
                                        waits=(tsm, kb_toks[hh], ztok[zq]))
                            tu, _ = vop(lambda eng, zq=zq: eng.tensor_scalar_max(
                                zring[:, 512 * zq:512 * zq + 512],
                                zring[:, 512 * zq:512 * zq + 512], 0.0))
                            t2_, _ = sop(lambda eng, zq=zq: eng.activation(
                                u2ring[:, 512 * zq:512 * zq + 512],
                                zring[:, 512 * zq:512 * zq + 512], AF.Square),
                                waits=(tu, u2tok[zq]))
                            ztok[zq] = t2_
                            ertok, _ = sop(lambda eng, zq=zq, e=e_ap:
                                           eng.activation(
                                               e, u2ring[:, 512 * zq:
                                                         512 * zq + 512],
                                               AF.Square),
                                           waits=(etok_slot[esl],))
                            u2tok[zq] = ertok
                        st["p512"][sring] = ertok
                        if 0 <= dmask <= 3:
                            ertok, _ = vop(lambda eng, e=e_ap, dmask=dmask:
                                           eng.tensor_mul(
                                               e, e,
                                               masksb[:, 512 * dmask:
                                                      512 * dmask + 512]),
                                           waits=(ertok, mask_tok))
                        pend.append((hh, tc, e_ap, ertok))
                    while len(pend) > 3:
                        hh_, tc_, ea_, et_ = pend.pop(0)
                        tpv = do_pv(hh_, tc_, ea_, et_)
                        etok_slot[(2 * tc_ + hh_) % 4] = tpv
                for hh_, tc_, ea_, et_ in pend:
                    tpv = do_pv(hh_, tc_, ea_, et_)
                    etok_slot[(2 * tc_ + hh_) % 4] = tpv
                for hh in range(2):
                    pvp = pv_ps[hh]
                    if L0:
                        tr1, _ = vop(lambda eng, pvp=pvp: eng.reciprocal(
                            rbuf[64:65, :], pvp[64:65, :]), waits=(pv_tok[hh],))
                    else:
                        tr0, _ = vop(lambda eng, pvp=pvp: eng.tensor_scalar_add(
                            rbuf[64:65, :], pvp[64:65, :], 1e-6),
                            waits=(pv_tok[hh],))
                        tr1, _ = vop(lambda eng: eng.reciprocal(
                            rbuf[64:65, :], rbuf[64:65, :]), sw=True)
                    trb, _ = mm(pA[0:64, 512:1024], ones2d[64:65, 0:64],
                                rbuf[64:65, :], True, True, sem="t",
                                waits=(tr1, st["pA"]))
                    tre, _ = vop(lambda eng: eng.tensor_copy(
                        recB[:, :], pA[0:64, 512:1024]), waits=(trb,))
                    to_, _ = vop(lambda eng, pvp=pvp, hh=hh, u=u, pair=pair:
                                 eng.tensor_mul(
                                     OT[64 * hh:64 * hh + 64,
                                        T * pair + 512 * u:
                                        T * pair + 512 * u + 512],
                                     pvp[0:64, :], recB[:, :]), sw=True)
                    if hh == 0:
                        st["p512"][4] = to_
                        st["pA"] = tre
                    else:
                        st["pA"] = [tre, to_]
                    ot_last = to_
        return ot_last

    def phase_outproj(layer, ot_tok):
        L0 = layer == 0
        nm = "wo" if L0 else "wp"
        last = None
        for f in range(8):
            for pk in range(8):
                lhs = ring.get((nm, (f, pk)))
                ltok = ring.load_tok
                lrec = None
                for u in range(NSUP):
                    ps = p512[u]
                    tok, lrec = mm(ps[:, :], lhs,
                                   OT[:, T * pk + 512 * u:T * pk + 512 * u + 512],
                                   pk == 0, pk == 7,
                                   sem="t" if pk == 7 else None,
                                   waits=(ot_tok, ltok, st["p512"][u]))
                    if pk == 7:
                        if L0:
                            ta, _ = vop(lambda eng, f=f, u=u, ps=ps:
                                        eng.tensor_add(xt(f, 512 * u, 512),
                                                       xt(f, 512 * u, 512),
                                                       ps[:, :]), waits=(tok,))
                        else:
                            ta, _ = vop(lambda eng, f=f, u=u, ps=ps:
                                        eng.scalar_tensor_tensor(
                                            xt(f, 512 * u, 512), ps[:, :],
                                            bp1[:, f:f + 1], xt(f, 512 * u, 512),
                                            ALU.add, ALU.add),
                                        waits=(tok, const_tok))
                        st["p512"][u] = ta
                        last = ta
                ring.used(lrec, tok)
        return last

    def phase_moe(xn_tok):
        comb_tok = None
        for tt in range(NTT):
            ps = p512[tt % 2]
            for c in range(NCH):
                tok, _ = mm(ps[:, 0:8], xn(c, 128 * tt, 128),
                            gw16[:, 8 * c:8 * c + 8], c == 0, c == NCH - 1,
                            sem="t" if c == NCH - 1 else None,
                            waits=(xn_tok, gw_tok, st["p512"][tt % 2]))
            lt = x2tmp[:, 0:8]
            m1 = x2tmp[:, 8:9]
            m2 = x2tmp[:, 9:10]
            nm1 = x2tmp[:, 10:11]
            w2c = x2tmp[:, 11:12]
            w1c = x2tmp[:, 12:13]
            eq1 = x2tmp[:, 16:24]
            msk = x2tmp[:, 24:32]
            eq2 = x2tmp[:, 32:40]
            t0, _ = vop(lambda eng, ps=ps, lt=lt: eng.tensor_copy(lt, ps[:, 0:8]),
                        waits=(tok,))
            st["p512"][tt % 2] = t0
            vop(lambda eng, lt=lt, m1=m1: eng.reduce_max(m1, lt, AX.X), sw=True)
            vop(lambda eng, lt=lt, m1=m1, eq1=eq1: eng.tensor_scalar(
                eq1, lt, m1, None, ALU.is_equal), sw=True)
            vop(lambda eng, lt=lt, eq1=eq1, msk=msk: eng.scalar_tensor_tensor(
                msk, eq1, -1e30, lt, ALU.mult, ALU.add), sw=True)
            t4, _ = vop(lambda eng, msk=msk, m2=m2: eng.reduce_max(m2, msk, AX.X),
                        sw=True)
            t4b, _ = vop(lambda eng, m1=m1, nm1=nm1: eng.tensor_scalar_mul(
                nm1, m1, -1.0), sw=True)
            t5, _ = sop(lambda eng, m2=m2, nm1=nm1, w2c=w2c: eng.activation(
                w2c, m2, AF.Sigmoid, bias=nm1, scale=1.0), waits=(t4, t4b))
            t6, _ = vop(lambda eng, w2c=w2c, w1c=w1c: eng.tensor_scalar(
                w1c, w2c, -1.0, 1.0, ALU.mult, ALU.add), waits=(t5,))
            vop(lambda eng, msk=msk, m2=m2, eq2=eq2: eng.tensor_scalar(
                eq2, msk, m2, None, ALU.is_equal), sw=True)
            vop(lambda eng, eq1=eq1, w1c=w1c, tt=tt: eng.tensor_scalar_mul(
                comb[:, 8 * tt:8 * tt + 8], eq1, w1c), sw=True)
            comb_tok, _ = vop(lambda eng, w2c=w2c, eq2=eq2, tt=tt:
                              eng.scalar_tensor_tensor(
                                  comb[:, 8 * tt:8 * tt + 8], eq2, w2c,
                                  comb[:, 8 * tt:8 * tt + 8],
                                  ALU.mult, ALU.add), sw=True)

        h_last_reader = None
        xadd = None
        for e in range(E + 1):
            shared = e == E
            cb_tok = None
            if not shared:
                crt = None
                for tt in range(NTT):
                    ttr, _ = emt("t", lambda eng, tt=tt, e=e: eng.transpose(
                        pA[0:1, 128 * (tt % 4):128 * (tt % 4) + 128],
                        comb[:, 8 * tt + e:8 * tt + e + 1], identf[:, :]),
                        sem="t", waits=(comb_tok, const_tok, st["pA"]))
                    if tt % 4 == 3:
                        crt, _ = vop(lambda eng, tt=tt: eng.tensor_copy(
                            crow16[0:1, 128 * (tt - 3):128 * (tt - 3) + 512],
                            pA[0:1, 0:512]), waits=(ttr,))
                        st["pA"] = crt
                for u in range(NSUP):
                    tcb, _ = mm(pA[:, 0:512], onesrow16[:, :],
                                crow16[0:1, 512 * u:512 * u + 512], True, True,
                                sem="t", waits=(crt, st["pA"]))
                    cb_tok, _ = vop(lambda eng, u=u: eng.tensor_copy(
                        combB[:, 512 * u:512 * u + 512], pA[:, 0:512]),
                        waits=(tcb,))
                    st["pA"] = cb_tok
            wg_nm, wu_nm, wd_nm = ("sg", "su", "sd") if shared else \
                ("wg", "wu", "wd")
            for half in range(2):
                uoff = 2 * half
                sg_tok = [None] * 4
                hw_tok = {}
                for f0 in range(0, 12, 2):
                    gstop = {}
                    for fi, f in enumerate((f0, f0 + 1)):
                        bs = 2 * fi
                        for k in range(NCH):
                            key = (wg_nm, (f, k)) if shared else \
                                (wg_nm, (e, f, k))
                            lhs = ring.get(key)
                            ltok = ring.load_tok
                            lrec = None
                            for ui in range(2):
                                tok, lrec = mm(
                                    p512[bs + ui][:, :], lhs,
                                    xn(k, 512 * (uoff + ui), 512),
                                    k == 0, k == NCH - 1,
                                    sem="t" if k == NCH - 1 else None,
                                    waits=(xn_tok, ltok,
                                           st["p512"][bs + ui]))
                                if k == NCH - 1:
                                    gstop[(fi, ui)] = tok
                            ring.used(lrec, tok if k == NCH - 1 else None)
                        for ui in range(2):
                            bk = bs + ui
                            ts_, _ = sop(lambda eng, bk=bk, sl=512 * bk:
                                         eng.activation(
                                             sgt[:, sl:sl + 512],
                                             p512[bk][:, :], AF.Silu),
                                         waits=(gstop[(fi, ui)], sg_tok[bk]))
                            st["p512"][bk] = ts_
                            hw_tok[(fi, ui, "silu")] = ts_
                    for fi, f in enumerate((f0, f0 + 1)):
                        bs = 2 * fi
                        ustop = {}
                        for k in range(NCH):
                            key = (wu_nm, (f, k)) if shared else \
                                (wu_nm, (e, f, k))
                            lhs = ring.get(key)
                            ltok = ring.load_tok
                            lrec = None
                            for ui in range(2):
                                tok, lrec = mm(
                                    p512[bs + ui][:, :], lhs,
                                    xn(k, 512 * (uoff + ui), 512),
                                    k == 0, k == NCH - 1,
                                    sem="t" if k == NCH - 1 else None,
                                    waits=(hw_tok[(fi, ui, "silu")]
                                           if k == 0 else None, ltok))
                                if k == NCH - 1:
                                    ustop[ui] = tok
                            ring.used(lrec, tok if k == NCH - 1 else None)
                        for ui in range(2):
                            bk = bs + ui
                            hsl = hbuf[:, 1024 * f + 512 * ui:
                                       1024 * f + 512 * ui + 512]
                            if shared:
                                th, _ = vop(lambda eng, bk=bk, hsl=hsl,
                                            sl=512 * bk: eng.tensor_mul(
                                                hsl, sgt[:, sl:sl + 512],
                                                p512[bk][:, :]),
                                            waits=(ustop[ui], h_last_reader))
                            else:
                                vop(lambda eng, sl=512 * bk, u=uoff + ui:
                                    eng.tensor_mul(
                                        x2tmp[:, 0:512], sgt[:, sl:sl + 512],
                                        combB[:, 512 * u:512 * u + 512]),
                                    waits=(hw_tok[(fi, ui, "silu")], cb_tok,
                                           st["x2"][0]))
                                th, _ = vop(lambda eng, bk=bk, hsl=hsl:
                                            eng.tensor_mul(
                                                hsl, x2tmp[:, 0:512],
                                                p512[bk][:, :]),
                                            waits=(ustop[ui], h_last_reader))
                            st["x2"][0] = th
                            sg_tok[bk] = th
                            st["p512"][bk] = th
                h_tok = th
                for o in range(8):
                    bs = 0 if o % 2 == 0 else 2
                    dstop = [None, None]
                    for j in range(12):
                        key = (wd_nm, (o, j)) if shared else \
                            (wd_nm, (e, o, j))
                        lhs = ring.get(key)
                        ltok = ring.load_tok
                        lrec = None
                        for ui in range(2):
                            td, lrec = mm(
                                p512[bs + ui][:, :], lhs,
                                hbuf[:, 1024 * j + 512 * ui:
                                     1024 * j + 512 * ui + 512],
                                j == 0, j == 11,
                                sem="t" if j == 11 else None,
                                waits=(h_tok, ltok, st["p512"][bs + ui]))
                            if j == 11:
                                dstop[ui] = td
                        ring.used(lrec, td)
                    h_last_reader = dstop[1]
                    for ui in range(2):
                        u = uoff + ui
                        ta2, _ = vop(lambda eng, o=o, u=u, bk=bs + ui:
                                     eng.tensor_add(
                                         xt(o, 512 * u, 512),
                                         xt(o, 512 * u, 512),
                                         p512[bk][:, :]),
                                     waits=(dstop[ui],))
                        st["p512"][bs + ui] = ta2
                        xadd = ta2
        return xadd

    def phase_ffn1(xn_tok):
        lr_tok = None
        for i in range(8):
            tokidx = 1 + 2 * i
            for half in range(2):
                ttr = None
                for cc in range(4):
                    c = 4 * half + cc
                    ttr, _ = emt("t", lambda eng, c=c, cc=cc, tokidx=tokidx:
                               eng.transpose(
                                   pbf[0:1, 128 * cc:128 * cc + 128],
                                   xnT[:, T * c + tokidx:T * c + tokidx + 1],
                                   identb[:, :]),
                               sem="t", waits=(xn_tok, const_tok, st["pbf"]))
                lr_tok, _ = vop(lambda eng, i=i, half=half: eng.tensor_copy(
                    lrows[0:1, 1024 * i + 512 * half:1024 * i + 512 * half + 512],
                    pbf[0:1, 0:512]), waits=(ttr,))
                st["pbf"] = lr_tok
        l4_tok = None
        for i4 in range(4):
            for q in range(4):
                l4_tok, _ = vop(lambda eng, i4=i4, q=q: eng.tensor_copy(
                    lrows4[0:1, 1024 * i4 + 256 * q:1024 * i4 + 256 * q + 256],
                    bass.AP(lrows, 1024 * (4 + i4) + q, [[8192, 1], [4, 256]])),
                    waits=(lr_tok,))
        xadd = None
        hgrp_reader = None
        dump_tok = None
        for g in range(4):
            odd_col = 2 + 2 * g
            a_tok = None
            for u in range(NSUP):
                tam = None
                for c in range(NCH):
                    tam, _ = mm(p512[4][0:1, :],
                                bass.AP(xnT, T * c + odd_col,
                                        [[8 * T, 128], [1, 1]]),
                                xn(c, 512 * u, 512), c == 0, c == NCH - 1,
                                sem="t" if c == NCH - 1 else None,
                                waits=(xn_tok, st["p512"][4]))
                a_tok, _ = vop(lambda eng, u=u: eng.tensor_scalar_mul(
                    arow[0:1, 512 * u:512 * u + 512], p512[4][0:1, :],
                    scgain[0:1, 0:1]), waits=(tam, const_tok))
                st["p512"][4] = a_tok
            c_tok = [None, None]
            for half in range(2):
                uoff = 2 * half
                h_tok = None
                for f0 in range(0, 8, 2):
                    for fi, f in enumerate((f0, f0 + 1)):
                        bs = 2 * fi
                        hstop = {}
                        for k in range(NCH):
                            lhs = ring.get(("w1", (g, f, k)))
                            ltok = ring.load_tok
                            lrec = None
                            for ui in range(2):
                                tok, lrec = mm(
                                    p512[bs + ui][:, :], lhs,
                                    xn(k, 512 * (uoff + ui), 512),
                                    k == 0, False,
                                    waits=(xn_tok, ltok, st["p512"][bs + ui],
                                           hgrp_reader if k == 0 else None))
                            ring.used(lrec, tok)
                        for ui in range(2):
                            tl, _ = mm(p512[bs + ui][:, :],
                                       lrows[0:1, 1024 * g + 128 * f:
                                             1024 * g + 128 * f + 128],
                                       arow[0:1, 512 * (uoff + ui):
                                            512 * (uoff + ui) + 512],
                                       False, True, sem="t",
                                       waits=(lr_tok, a_tok))
                            hstop[ui] = tl
                        for ui in range(2):
                            h_tok, _ = sop(
                                lambda eng, bk=bs + ui, g=g, f=f, ui=ui:
                                eng.activation(
                                    hgrp[:, 1024 * f + 512 * ui:
                                         1024 * f + 512 * ui + 512],
                                    p512[bk][:, :], AF.Gelu,
                                    bias=b1c[:, 8 * g + f:8 * g + f + 1]),
                                waits=(hstop[ui], const_tok, hgrp_reader))
                            st["p512"][bs + ui] = h_tok
                if CFG["dbgh"] and half == 1:
                    dcb = nc.dram_tensor("dbg_h_%d" % g,
                                         [128, 8192], BF16,
                                         kind="ExternalOutput").ap()
                    dump_tok, _ = ydma(dcb, hgrp[:, :], sem="out",
                                       waits=(h_tok,))
                for i4 in range(4):
                    odd2 = 10 + 2 * i4
                    for ui in range(2):
                        u = uoff + ui
                        tcm = None
                        for j in range(NCH):
                            tcm, _ = mm(p512[4][0:1, :],
                                        bass.AP(xnT, T * j + odd2,
                                                [[8 * T, 128], [1, 1]]),
                                        hgrp[:, 1024 * j + 512 * ui:
                                             1024 * j + 512 * ui + 512],
                                        j == 0, j == NCH - 1,
                                        sem="t" if j == NCH - 1 else None,
                                        waits=(h_tok, st["p512"][4]))
                        ct, _ = vop(lambda eng, i4=i4, ui=ui:
                                    eng.tensor_scalar_mul(
                                        crow4[0:1, 1024 * i4 + 512 * ui:
                                              1024 * i4 + 512 * ui + 512],
                                        p512[4][0:1, :],
                                        scgain[0:1, 0:1]),
                                    waits=(tcm,))
                        st["p512"][4] = ct
                c_tok[half] = ct
                for o in range(8):
                    bs = 0 if o % 2 == 0 else 2
                    OL = CFG["olora"]
                    for j in range(NCH):
                        lhs = ring.get(("w2", (g, o, j)))
                        ltok = ring.load_tok
                        lrec = None
                        for ui in range(2):
                            tok, lrec = mm(
                                p512[bs + ui][:, :], lhs,
                                hgrp[:, 1024 * j + 512 * ui:
                                     1024 * j + 512 * ui + 512],
                                j == 0, False if OL else j == NCH - 1,
                                sem=None if OL or j < NCH - 1 else "t",
                                waits=(h_tok, ltok, st["p512"][bs + ui]))
                        ring.used(lrec, tok)
                    hgrp_reader = [p.tok("t"), dump_tok]
                    i4 = o // 2
                    lsl = lrows4[0:1, 1024 * i4 + 256 * g + 128 * (o % 2):
                                 1024 * i4 + 256 * g + 128 * (o % 2) + 128]
                    for ui in range(2):
                        u = uoff + ui
                        if OL:
                            tl2, _ = mm(p512[bs + ui][:, :], lsl,
                                        crow4[0:1, 1024 * i4 + 512 * ui:
                                              1024 * i4 + 512 * ui + 512],
                                        False, True, sem="t",
                                        waits=(c_tok[half], l4_tok))
                        else:
                            tl2 = tok if ui == 1 else p.tok("t")
                        if g == 3:
                            ta, _ = vop(lambda eng, o=o, u=u, bk=bs + ui:
                                        eng.scalar_tensor_tensor(
                                            xt(o, 512 * u, 512),
                                            p512[bk][:, :],
                                            b2c[:, o:o + 1],
                                            xt(o, 512 * u, 512),
                                            ALU.add, ALU.add),
                                        waits=(tl2, const_tok))
                        else:
                            ta, _ = vop(lambda eng, o=o, u=u, bk=bs + ui:
                                        eng.tensor_add(
                                            xt(o, 512 * u, 512),
                                            xt(o, 512 * u, 512),
                                            p512[bk][:, :]),
                                        waits=(tl2,))
                        st["p512"][bs + ui] = ta
                        xadd = ta
        return xadd

    prev_store = None
    for sq in range(CFG["seqs"]):
        xl_tok, _ = ydma(xT[:, :], xv(sq), sem="xl",
                         waits=(prev_store, st["x_readers"]))
        st["x_ready"] = xl_tok
        for layer in range(CFG["layers"]):
            ln_tok = phase_ln(layer)
            toks = [ln_tok]
            if CFG["attn"]:
                ot_tok = phase_attention(layer, ln_tok)
                if CFG["outproj"]:
                    toks.append(phase_outproj(layer, ot_tok))
                else:
                    toks.append(ot_tok)
            if CFG["ffn"]:
                if layer == 0:
                    toks.append(phase_moe(ln_tok))
                else:
                    toks.append(phase_ffn1(ln_tok))
            st["x_ready"] = toks
        st_tok, _ = ydma(yv(sq), xT[:, :], sem="out", waits=(st["x_ready"],))
        prev_store = st_tok
        st["x_readers"] = st_tok
    ring.finish()
    n_out = CFG["seqs"] + (4 if CFG["dbgh"] else 0)
    if CFG["dbg"]:
        dbgs = [("xnT", xnT, BF16), ("qT", qT, BF16), ("kT", kT, BF16),
                ("vP", vP, BF16), ("OT", OT, BF16), ("mg", mg_sb, F32),
                ("er", ering, BF16), ("kcol", kcol, F32),
                ("mrow", mrow16, BF16), ("krow", krow16, BF16),
                ("mcol", mcolneg, F32), ("hb", hbuf, BF16),
                ("comb", comb, F32), ("sc2", scratch2, BF16),
                ("mu01", mu01, F32), ("lnrow", lnrow, F32)]
        allw = (p.tok("t"), p.tok("v"), p.tok("s"))
        for nm, h, dt_ in dbgs:
            shp = list(h.shape)
            d = nc.dram_tensor("dbg_" + nm, shp, dt_, kind="ExternalOutput").ap()
            ydma(d, h[:, :], sem="out", waits=allw)
            n_out += 1
    p.emit("y", lambda eng: eng.wait_ge(sems["out"], 16 * n_out))

    for e_ in "tvsgy":
        for _fn, _incs in p.q[e_]:
            assert len(_incs) <= 1, (e_, _incs)
    with nc.Block() as block:
        def replay(eng, lst):
            for fn, incs in lst:
                ins = fn(eng)
                for h, a in incs:
                    ins.then_inc(h, a)

        @block.tensor
        def _(eng):
            replay(eng, p.q["t"])

        @block.vector
        def _(eng):
            replay(eng, p.q["v"])

        @block.scalar
        def _(eng):
            replay(eng, p.q["s"])

        @block.gpsimd
        def _(eng):
            replay(eng, p.q["g"])

        @block.sync
        def _(eng):
            replay(eng, p.q["y"])

    es.close()
    return nc, ring.plan


def _host_tables():
    theta = np.exp(np.float32(np.log(np.float32(10000.0))) *
                   np.arange(HD // 2, dtype=np.float32) /
                   np.float32(HD)).astype(np.float32)
    args = (np.arange(T, dtype=np.float32)[:, None] * theta[None, :]).astype(np.float32)
    c = np.cos(args).astype(np.float32)
    s = np.sin(args).astype(np.float32)
    ropeC = np.zeros((128, T), dtype=np.float32)
    ropeS = np.zeros((128, T), dtype=np.float32)
    for r in range(128):
        d = r % HD
        j = d // 2
        ropeC[r] = c[:, j]
        ropeS[r] = (-s[:, j]) if (d % 2 == 0) else s[:, j]
    masks = np.zeros((4, 128, 512), dtype=BF)
    for dd in range(4):
        i = np.arange(128)[:, None]
        j = np.arange(512)[None, :]
        masks[dd] = (j >= i + 128 * dd).astype(BF)
    pswap = np.zeros((128, 128), dtype=BF)
    for i in range(128):
        pswap[i ^ 1, i] = 1.0
    return ropeC.astype(np.float16), ropeS.astype(np.float16), masks, pswap


def _chunk_weights(inp):
    w = {}

    def colchunk(a):
        return np.ascontiguousarray(
            a.reshape(a.shape[0] // 128, 128, a.shape[1] // 128, 128)
            .transpose(0, 2, 3, 1))

    w["wq"] = colchunk(np.asarray(inp["L0_attn_wq"]))
    w["wk"] = colchunk(np.asarray(inp["L0_attn_wk"]))
    w["wv"] = colchunk(np.asarray(inp["L0_attn_wv"]))
    w["wo"] = colchunk(np.asarray(inp["L0_attn_wo"]))
    wu_all = np.asarray(inp["L0_ffn_w_up"])
    w["wg"] = np.ascontiguousarray(
        wu_all[:, :HALF, :].reshape(E, 12, 128, 8, 128).transpose(0, 1, 3, 4, 2))
    w["wu"] = np.ascontiguousarray(
        wu_all[:, HALF:, :].reshape(E, 12, 128, 8, 128).transpose(0, 1, 3, 4, 2))
    w["wd"] = np.ascontiguousarray(
        np.asarray(inp["L0_ffn_w_down"]).reshape(E, 8, 128, 12, 128)
        .transpose(0, 1, 3, 4, 2))
    w["sg"] = np.ascontiguousarray(
        np.asarray(inp["L0_ffn_sg_gate"]).reshape(12, 128, 8, 128)
        .transpose(0, 2, 3, 1))
    w["su"] = np.ascontiguousarray(
        np.asarray(inp["L0_ffn_sg_up"]).reshape(12, 128, 8, 128)
        .transpose(0, 2, 3, 1))
    w["sd"] = np.ascontiguousarray(
        np.asarray(inp["L0_ffn_sg_down"]).reshape(8, 128, 12, 128)
        .transpose(0, 2, 3, 1))
    wat = np.asarray(inp["L1_attn_W_attn"])
    for i, nm in enumerate(("q1", "k1", "v1")):
        blk = wat[:, 1024 * i:1024 * (i + 1)]
        w[nm] = np.ascontiguousarray(
            blk.reshape(8, 128, 8, 128).transpose(2, 0, 1, 3))
    w["wp"] = np.ascontiguousarray(
        np.asarray(inp["L1_attn_W_proj"]).reshape(8, 128, 8, 128)
        .transpose(2, 0, 1, 3))
    w["w1"] = np.ascontiguousarray(
        np.asarray(inp["L1_ffn_W1"]).reshape(4, 8, 128, 8, 128)
        .transpose(0, 1, 3, 4, 2))
    w["w2"] = np.ascontiguousarray(
        np.asarray(inp["L1_ffn_W2"]).reshape(8, 128, 4, 8, 128)
        .transpose(2, 0, 3, 4, 1))
    return w


_CACHE = {}


def kernel(**inputs):
    if "prog" not in _CACHE:
        _CACHE["prog"] = build()
    nc, plan = _CACHE["prog"]
    if CFG["seqs"] == SPC and CFG["layers"] == 2 and CFG["attn"] \
            and CFG["outproj"] and CFG["ffn"]:
        assert len(plan) == 2 * NCHUNK, len(plan)
        assert plan[:NCHUNK] == plan[NCHUNK:]
    w = _chunk_weights(inputs)
    nuse = min(len(plan), NCHUNK)
    stream = np.zeros((NCHUNK, 128, 128), dtype=np.float32)
    for i, (nm, idx) in enumerate(plan[:nuse]):
        stream[i] = w[nm][idx]
    wg = np.ascontiguousarray(
        stream.reshape(NG, GROUP, 128, 128).transpose(0, 2, 1, 3)
        .reshape(NG, 128, GROUP * 128))

    ropeC, ropeS, masks, pswap = _host_tables()
    ba = np.asarray(inputs["L1_attn_b_attn"], dtype=np.float32)
    bqk = np.ascontiguousarray(np.concatenate(
        [ba[0:1024].reshape(8, 128).T, ba[1024:2048].reshape(8, 128).T], axis=1))
    common = {
        "wstr": wg,
        "d_ropeC": ropeC, "d_ropeS": ropeS, "d_masks": masks,
        "d_pswap": pswap,
        "d_identb": np.eye(128, dtype=BF),
        "d_identf": np.eye(128, dtype=np.float32),
        "d_onesrowf": np.ones((1, 128), dtype=np.float32),
        "d_onesrow16": np.ones((1, 128), dtype=BF),
        "d_onescol16": np.ones((128, 1), dtype=BF),
        "d_onescolf": np.ones((128, 1), dtype=np.float32),
        "d_ones2d": np.ones((128, 64), dtype=np.float32),
        "d_gw": np.ascontiguousarray(
            np.asarray(inputs["L0_ffn_gate_w"], dtype=np.float32).T
            .reshape(8, 128, 8)),
        "d_bqk": bqk,
        "d_bv1r": np.ascontiguousarray(ba[2048:3072].reshape(1, 1024)).astype(BF),
        "d_bp1": np.ascontiguousarray(
            np.asarray(inputs["L1_attn_b_proj"], dtype=np.float32)
            .reshape(8, 128).T),
        "d_b1c": np.ascontiguousarray(
            np.asarray(inputs["L1_ffn_b1"], dtype=np.float32).reshape(32, 128).T),
        "d_b2c": np.ascontiguousarray(
            np.asarray(inputs["L1_ffn_b2"], dtype=np.float32).reshape(8, 128).T),
        "d_sc": np.array([[np.float32(inputs["L1_ffn_scale"]),
                           np.float32(inputs["L1_attn_score_gain"])]],
                         dtype=np.float32),
    }
    x = np.asarray(inputs["L0_attn_x"], dtype=np.float32)
    in_maps = []
    for c in range(NCORES):
        m = dict(common)
        m["x_in"] = np.ascontiguousarray(
            x[SPC * c:SPC * (c + 1)].transpose(0, 2, 1))
        in_maps.append(m)
    res = run_bass_kernel_spmd(nc, in_maps, core_ids=list(range(NCORES)))
    out = np.empty((B, T, D), dtype=np.float32)
    for c in range(NCORES):
        y = res.results[c]["y_out"]
        out[SPC * c:SPC * (c + 1)] = np.asarray(y).transpose(0, 2, 1)
    return out
